# version 06
# baseline (speedup 1.0000x reference)
import numpy as np
import ml_dtypes
from contextlib import ExitStack

import concourse.bass as bass
import concourse.mybir as mybir
from concourse.bass_utils import run_bass_kernel_spmd

F32 = mybir.dt.float32
BF16 = mybir.dt.bfloat16
AF = mybir.ActivationFunctionType
OP = mybir.AluOpType

D = 1024
T = 4096
B = 4
NPAIR = 8
NKT = 32
NTP = 16
NTILE = 16
TW = 256
UW = 512
EPS = 1e-5
RD = 1.0 / 1024.0
UNITS = [[0, 1, 6, 7], [2, 3, 4, 5]]


def _build(detect_races=True):
    nc = bass.Bass("TRN2", target_bir_lowering=False,
                   detect_race_conditions=detect_races)

    xT = nc.dram_tensor("xT", [D, T], F32, kind="ExternalInput").ap()
    wfr = nc.dram_tensor("wfr", [D, D], F32, kind="ExternalInput").ap()
    wfi = nc.dram_tensor("wfi", [D, D], F32, kind="ExternalInput").ap()
    c1s = nc.dram_tensor("c1s", [6, D, 128], BF16, kind="ExternalInput").ap()
    c2s = nc.dram_tensor("c2s", [4, D, D], BF16, kind="ExternalInput").ap()
    wq = nc.dram_tensor("wq", [64, 64], F32, kind="ExternalInput").ap()
    wk = nc.dram_tensor("wk", [64, 64], F32, kind="ExternalInput").ap()
    wvT = nc.dram_tensor("wvT", [64, 64], F32, kind="ExternalInput").ap()
    WuT = nc.dram_tensor("WuT", [D, D], F32, kind="ExternalInput").ap()
    buT = nc.dram_tensor("buT", [128, 8], F32, kind="ExternalInput").ap()
    fbT = nc.dram_tensor("fbT", [128, 8], F32, kind="ExternalInput").ap()
    tri = nc.dram_tensor("tri", [128, 128], F32, kind="ExternalInput").ap()
    out = nc.dram_tensor("out", [D, 2048], F32, kind="ExternalOutput").ap()

    x2h = nc.dram_tensor("x2h", [D, T], F32, kind="Internal").ap()
    xl1h = nc.dram_tensor("xl1h", [D, T], BF16, kind="Internal").ap()
    xl1l = nc.dram_tensor("xl1l", [D, T], BF16, kind="Internal").ap()
    keysh = nc.dram_tensor("keysh", [NPAIR, NTP, 128, 256], BF16, kind="Internal").ap()
    vh = nc.dram_tensor("vh", [NPAIR, NKT, 128, 130], BF16, kind="Internal").ap()
    wuh = nc.dram_tensor("wuh", [8, 128, 8, 128], BF16, kind="Internal").ap()
    ccin = nc.dram_tensor("ccin", [128, D], F32, kind="Internal")
    ccout = nc.dram_tensor("ccout", [D, D], F32, kind="Internal")

    eye_in = nc.inline_tensor(np.eye(128, dtype=ml_dtypes.bfloat16), name="eyebf").ap()
    ones_in = nc.inline_tensor(np.ones((128, 128), dtype=ml_dtypes.bfloat16),
                               name="onesbf").ap()

    es = ExitStack()

    def sb(name, shape, dt):
        return es.enter_context(nc.sbuf_tensor(name, shape, dt))

    G2 = sb("G2", [128, 128], BF16)
    Wv2 = sb("Wv2", [128, 128], BF16)
    eye_sb = sb("eye_sb", [128, 128], BF16)
    ones_sb = sb("ones_sb", [128, 128], BF16)
    tri_sb = sb("tri_sb", [128, 128], F32)
    buT_sb = sb("buT_sb", [128, 8], F32)
    fb2_sb = sb("fb2_sb", [128, 8], F32)
    _m1r = sb("m1r", [33, 512], F32)
    _v1r = sb("v1r", [33, 512], F32)
    _q1r = sb("q1r", [33, 512], F32)
    _r1r = sb("r1r", [33, 512], F32)
    _mh1 = sb("mh1", [33, 512], BF16)
    _rh1 = sb("rh1", [33, 512], BF16)
    _rl1 = sb("rl1", [33, 512], BF16)
    m1r, v1r, q1r, r1r = _m1r[32:33, :], _v1r[32:33, :], _q1r[32:33, :], _r1r[32:33, :]
    mh1, rh1, rl1 = _mh1[32:33, :], _rh1[32:33, :], _rl1[32:33, :]
    _rzr = sb("rzr", [65, 1024], BF16)
    _rzf = sb("rzf", [65, 1024], F32)
    rzr, rzf = _rzr[64:65, :], _rzf[64:65, :]
    xt = [sb("xt0", [128, 8, TW], F32), sb("xt1", [128, 8, TW], F32)]
    xbf = sb("xbf", [128, 8, TW], BF16)
    xls = sb("xls", [128, 8, TW], BF16)
    xf1 = sb("xf1", [128, 8, TW], F32)


    prep_psum = ExitStack()
    Aps = prep_psum.enter_context(nc.psum_tensor("Aps", [128, 1024], F32))
    Bps = prep_psum.enter_context(nc.psum_tensor("Bps", [128, 1024], F32))
    Mps = prep_psum.enter_context(nc.psum_tensor("Mps", [128, 1024], F32))
    tpa = prep_psum.enter_context(nc.psum_tensor("tpa", [128, 128], BF16))
    tpb = prep_psum.enter_context(nc.psum_tensor("tpb", [128, 128], BF16))

    class Sem:
        def __init__(self, name):
            self.h = es.enter_context(nc.semaphore(name))
            self.v = 0

    s_ld = Sem("s_ld")
    s_sd = Sem("s_sd")
    s_gd = Sem("s_gd")
    s_cc = Sem("s_cc")
    s_pe = Sem("s_pe")
    s_dv = Sem("s_dv")
    s_ac = Sem("s_ac")

    plan = {e: [] for e in ("sync", "tensor", "vector", "scalar", "gpsimd")}

    def on(e, fn):
        plan[e].append(fn)

    def wait(e, sem, val):
        if val and val > 0:
            plan[e].append(lambda eng, s=sem.h, v=val: eng.wait_ge(s, v))

    def emit(e, fn, sem=None, k=1):
        if sem is None:
            plan[e].append(fn)
            return None
        sem.v += k
        val = sem.v
        plan[e].append(lambda eng, f=fn, s=sem.h, kk=k: f(eng).then_inc(s, kk))
        return val

    prep_sbuf = ExitStack()
    whl = prep_sbuf.enter_context(nc.sbuf_tensor("whl", [128, 4, 8, 1024], BF16))
    crts = prep_sbuf.enter_context(nc.sbuf_tensor("crts", [128, 6, 8, 128], BF16))
    ABhl = prep_sbuf.enter_context(nc.sbuf_tensor("ABhl", [128, 4, 1024], BF16))
    ABthl = prep_sbuf.enter_context(nc.sbuf_tensor("ABthl", [128, 4, 1024], BF16))
    wsm = prep_sbuf.enter_context(nc.sbuf_tensor("wsm", [64, 3, 64], BF16))
    msl = prep_sbuf.enter_context(nc.sbuf_tensor("msl", [128, 1024], F32))

    for dst, s_ in ((tri_sb[:], tri), (buT_sb[:], buT), (fb2_sb[:], fbT),
                    (eye_sb[:], eye_in), (ones_sb[:], ones_in)):
        emit("sync", lambda e, d=dst, s=s_: e.dma_start(out=d, in_=s), s_ld, 16)
    ld_small = s_ld.v

    for dst, s_ in (
        (crts[:], c1s.rearrange("s (c p) m -> p s c m", p=128)),
        (wsm[:, 0, :], wq), (wsm[:, 1, :], wk), (wsm[:, 2, :], wvT),
    ):
        emit("gpsimd", lambda e, d=dst, s=s_: e.dma_start(out=d, in_=s), s_gd, 16)
    gd_w = s_gd.v

    with nc.sbuf_tensor("wf32", [128, 8, 1024], F32) as wf32:
        dv_wsp = 0
        for wi, wsrc in ((0, wfr), (1, wfi)):
            if dv_wsp:
                wait("sync", s_dv, dv_wsp)
            ldw = emit("sync", lambda e, s=wsrc: e.dma_start(
                out=wf32[:], in_=s.rearrange("(c p) n -> p c n", p=128)), s_ld, 16)
            wait("vector", s_ld, ldw)
            on("vector", lambda e, w=wi: e.tensor_copy(
                whl[:, 2 * w, :, :].rearrange("p c n -> p (c n)"),
                wf32[:].rearrange("p c n -> p (c n)")))
            dv_wsp = emit("vector", lambda e, w=wi: e.tensor_sub(
                whl[:, 2 * w + 1, :, :].rearrange("p c n -> p (c n)"),
                wf32[:].rearrange("p c n -> p (c n)"),
                whl[:, 2 * w, :, :].rearrange("p c n -> p (c n)")), s_dv)

    wait("tensor", s_gd, gd_w)
    pe_g = emit("tensor", lambda e: e.matmul(Mps[0:64, 0:64], wsm[:, 0, :],
                                             wsm[:, 1, :], start=True, stop=True), s_pe)
    wait("vector", s_gd, gd_w)
    wait("vector", s_ld, ld_small)
    on("vector", lambda e: e.memset(G2[:], 0.0))
    on("vector", lambda e: e.memset(Wv2[:], 0.0))
    wait("vector", s_pe, pe_g)
    on("vector", lambda e: e.tensor_scalar_mul(G2[0:64, 0:64], Mps[0:64, 0:64],
                                               1.0 / 32.0))
    on("vector", lambda e: e.tensor_copy(Wv2[0:64, 0:64], wsm[:, 2, :]))
    on("vector", lambda e: e.tensor_scalar_mul(fb2_sb[:], fb2_sb[:], 0.5))
    dv_g2 = emit("vector", lambda e: e.engine_nop(), s_dv)
    wait("sync", s_dv, dv_g2)
    emit("sync", lambda e: e.dma_start(out=G2[64:128, 64:128],
                                       in_=G2[0:64, 0:64]), s_ld, 16)
    gdiag_ld = emit("sync", lambda e: e.dma_start(out=Wv2[64:128, 64:128],
                                                  in_=Wv2[0:64, 0:64]), s_ld, 16)
    wait("tensor", s_ld, gdiag_ld)

    wait("tensor", s_dv, dv_wsp)

    def st1(e):
        last = None
        for dst_ps, terms in (
            (Aps, ((0, 0), (0, 1), (1, 0), (2, 2), (2, 3), (3, 2))),
            (Bps, ((0, 2), (0, 3), (1, 2), (4, 0), (4, 1), (5, 0))),
        ):
            for n2 in range(2):
                nsl = slice(512 * n2, 512 * n2 + 512)
                nterm = len(terms)
                for ti, (ci, wi) in enumerate(terms):
                    for kc in range(8):
                        last = e.matmul(dst_ps[:, nsl], crts[:, ci, kc, :],
                                        whl[:, wi, kc, nsl],
                                        start=(ti == 0 and kc == 0),
                                        stop=(ti == nterm - 1 and kc == 7))
        return last
    pe_st1 = emit("tensor", st1, s_pe)

    wait("vector", s_pe, pe_st1)
    on("vector", lambda e: e.tensor_copy(ABhl[:, 0, :], Aps[:]))
    on("vector", lambda e: e.tensor_sub(ABhl[:, 1, :], Aps[:], ABhl[:, 0, :]))
    on("vector", lambda e: e.tensor_copy(ABhl[:, 2, :], Bps[:]))
    dv_ab = emit("vector", lambda e: e.tensor_sub(ABhl[:, 3, :], Bps[:],
                                                  ABhl[:, 2, :]), s_dv)

    wait("tensor", s_dv, dv_ab)
    tp_marks = []
    for i in range(32):
        m, kc = divmod(i, 8)
        tp = tpa if i % 2 == 0 else tpb
        if i >= 2:
            wait("tensor", s_dv, tp_marks[i - 2])
        pe_t = emit("tensor", lambda e, t=tp, mm=m, k=kc: e.transpose(
            t[:], ABhl[:, mm, 128 * k:128 * k + 128], eye_sb[:]), s_pe)
        wait("vector", s_pe, pe_t)
        dvm = emit("vector", lambda e, t=tp, mm=m, k=kc: e.tensor_copy(
            ABthl[:, mm, 128 * k:128 * k + 128], t[:]), s_dv)
        tp_marks.append(dvm)
    dv_tp = s_dv.v

    import kernel as _kmA
    if getattr(_kmA, "DEBUG_DUMP", False):
        dab = nc.dram_tensor("dbg_ab", [128, 4, 1024], BF16, kind="ExternalOutput").ap()
        wait("gpsimd", s_dv, dv_ab)
        emit("gpsimd", lambda e: e.dma_start(out=dab, in_=ABhl[:]), s_gd, 16)
        dabt = nc.dram_tensor("dbg_abt", [128, 4, 1024], BF16, kind="ExternalOutput").ap()
        wait("gpsimd", s_dv, dv_tp)
        emit("gpsimd", lambda e: e.dma_start(out=dabt, in_=ABthl[:]), s_gd, 16)
        dwhl = nc.dram_tensor("dbg_whl", [128, 4, 8, 1024], BF16, kind="ExternalOutput").ap()
        wait("gpsimd", s_dv, dv_wsp)
        emit("gpsimd", lambda e: e.dma_start(out=dwhl, in_=whl[:]), s_gd, 16)
    wait("gpsimd", s_pe, pe_st1)
    gd_c2 = emit("gpsimd", lambda e: e.dma_start(
        out=whl[:], in_=c2s.rearrange("s (c p) n -> p s c n", p=128)), s_gd, 16)
    wait("tensor", s_gd, gd_c2)
    wait("tensor", s_dv, dv_tp)

    def st2(e):
        last = None
        terms2 = ((0, 0), (0, 1), (1, 0), (2, 2), (2, 3), (3, 2))
        for n2 in range(2):
            nsl = slice(512 * n2, 512 * n2 + 512)
            for ti, (ai, ci) in enumerate(terms2):
                for kc in range(8):
                    last = e.matmul(Mps[:, nsl],
                                    ABthl[:, ai, 128 * kc:128 * kc + 128],
                                    whl[:, ci, kc, nsl],
                                    start=(ti == 0 and kc == 0),
                                    stop=(ti == 5 and kc == 7))
        return last
    pe_st2 = emit("tensor", st2, s_pe)

    wait("vector", s_pe, pe_st2)
    dv_m = emit("vector", lambda e: e.tensor_copy(msl[:], Mps[:]), s_dv)
    wait("gpsimd", s_dv, dv_m)
    emit("gpsimd", lambda e: e.dma_start(out=ccin.ap(), in_=msl[:]), s_gd, 16)
    wait("gpsimd", s_gd, s_gd.v)
    s_cc.v += 16
    on("gpsimd", lambda e: e.collective_compute(
        "AllGather", OP.bypass, replica_groups=[list(range(8))],
        ins=[ccin.ap().opt()], outs=[ccout.ap().opt()]).then_inc(s_cc.h))
    wait("gpsimd", s_cc, 1)

    wait("gpsimd", s_pe, pe_st2)
    emit("gpsimd", lambda e: e.dma_start(
        out=whl[:, 0, :, :], in_=WuT.rearrange("(c p) n -> p c n", p=128)), s_gd, 16)
    wait("gpsimd", s_gd, s_gd.v)
    gd_wu = emit("gpsimd", lambda e: e.dma_start(
        out=wuh.rearrange("o p c t -> p c o t"),
        in_=whl[:, 0, :, :].rearrange("p c (o t) -> p c o t", o=8)), s_gd, 16)

    prep_sbuf.close()
    prep_psum.close()

    Mt = sb("Mt", [128, 8, 1024], BF16)
    Mt_l = sb("Mt_l", [128, 8, 1024], BF16)
    mtf32 = sb("mtf32", [128, 8, 1024], F32)
    gd_mt = emit("gpsimd", lambda e: e.dma_start(
        out=Mt[:], in_=ccout.ap().rearrange("(c p) n -> p c n", p=128)), s_gd, 16)
    gd_mtf = emit("gpsimd", lambda e: e.dma_start(
        out=mtf32[:], in_=ccout.ap().rearrange("(c p) n -> p c n", p=128)), s_gd, 16)
    wait("vector", s_gd, gd_mtf)
    dv_mtl = emit("vector", lambda e: e.tensor_sub(
        Mt_l[:].rearrange("p c n -> p (c n)"),
        mtf32[:].rearrange("p c n -> p (c n)"),
        Mt[:].rearrange("p c n -> p (c n)")), s_dv)

    kstg = sb("kstg", [128, 8, TW], BF16)
    vstg = sb("vstg", [128, 2, 8, 2, 65], BF16)
    kst = sb("kst", [128, NTP, 256], BF16)
    vsw = sb("vsw", [128, NKT, 2, 65], BF16)
    q_sb = sb("q_sb", [128, 512], BF16)
    P_sb = [sb("P0", [128, 1024], BF16), sb("P1", [128, 1024], BF16)]
    osb = sb("osb", [128, 1024], F32)
    on_sb = sb("on_sb", [128, 8, 512], BF16)
    x3 = sb("x3", [128, 8, 512], F32)
    xb3 = sb("xb3", [128, 8, 512], BF16)
    xbl3 = sb("xbl3", [128, 8, 512], BF16)
    x2r = [sb("x2r0", [128, 512], F32), sb("x2r1", [128, 512], F32)]
    o4 = [sb("o40", [128, 512], F32), sb("o41", [128, 512], F32)]
    wut = [sb("wut0", [128, 8, 128], BF16), sb("wut1", [128, 8, 128], BF16)]
    tmpB = [sb("tmpB0", [64, 512], BF16), sb("tmpB1", [64, 512], BF16)]

    PA = es.enter_context(nc.psum_tensor("PA", [128, 1024], F32))
    PB = es.enter_context(nc.psum_tensor("PB", [128, 1024], F32))
    PC = es.enter_context(nc.psum_tensor("PC", [128, 1024], F32))
    PD = es.enter_context(nc.psum_tensor("PD", [128, 1024], F32))
    LXs, LXb = PD[:, 0:512], PD[:, 512:1024]
    LYs, LYb = PC[:, 0:512], PC[:, 512:1024]
    Y1v = [PB[:, 0:256], PB[:, 512:768]]
    VPv = [PA[:, 0:512], PA[:, 512:1024]]
    SC = [PA, PB]
    oAv = PC[0:65, 0:512]
    oBv = PC[0:65, 512:1024]
    pjv = PD[:, 0:512]
    L3v = PD[:, 512:1024]

    wait("vector", s_gd, gd_wu)
    for _kh in range(2):
        for _pr in range(8):
            for _hh in range(2):
                on("vector", lambda e, a=_kh, b=_pr, c=_hh: e.memset(
                    vstg[:, a, b, c, 64:65], 1.0))
    dv_vones = emit("vector", lambda e: e.engine_nop(), s_dv)
    mark_x2out = [0] * NTILE
    mark_xl1out = [0] * NTILE
    mark_kout = [0] * NTILE
    mark_vout = [0] * NTILE
    mark_finX = [0] * NTILE
    mark_appX = [0] * NTILE
    mark_finY = [0] * NTILE
    mark_appY = [0] * NTILE
    mark_vevac = [0] * NTILE
    mark_peEnd = [0] * NTILE

    def emit_ln(xsrc, bank_s, bank_b, out_dst, fin_prev, app_prev, split=False):
        dv_cast = emit("vector", lambda e: e.tensor_copy(
            xbf[:].rearrange("p c t -> p (c t)"),
            xsrc[:].rearrange("p c t -> p (c t)")), s_dv)
        wait("tensor", s_dv, dv_cast)
        if fin_prev:
            wait("tensor", s_dv, fin_prev)

        def chain(e, half):
            last = None
            for kc in range(8):
                last = e.matmul(bank_s[32:33, TW * half:TW * half + TW],
                                ones_sb[:, 0:1],
                                xbf[:, kc, :], start=(kc == 0), stop=(kc == 7),
                                tile_position=(0, 32))
            return last
        pe_c1 = emit("tensor", lambda e: chain(e, 0), s_pe)
        wait("vector", s_pe, pe_c1)
        dv_sq = emit("vector", lambda e: e.tensor_mul(
            xbf[:].rearrange("p c t -> p (c t)"),
            xbf[:].rearrange("p c t -> p (c t)"),
            xbf[:].rearrange("p c t -> p (c t)")), s_dv)
        wait("tensor", s_dv, dv_sq)
        pe_c2 = emit("tensor", lambda e: chain(e, 1), s_pe)
        wait("vector", s_pe, pe_c2)
        on("vector", lambda e: e.tensor_scalar_mul(m1r[:, 0:TW], bank_s[32:33, 0:TW], RD))
        on("vector", lambda e: e.tensor_scalar_mul(v1r[:, 0:TW], bank_s[32:33, TW:2 * TW], RD))
        on("vector", lambda e: e.tensor_mul(q1r[:, 0:TW], m1r[:, 0:TW], m1r[:, 0:TW]))
        fin = emit("vector", lambda e: e.tensor_sub(v1r[:, 0:TW], v1r[:, 0:TW],
                                                    q1r[:, 0:TW]), s_dv)
        on("vector", lambda e: e.tensor_scalar_add(v1r[:, 0:TW], v1r[:, 0:TW], EPS))
        wait("scalar", s_dv, fin)
        on("scalar", lambda e: e.activation(v1r[:, 0:TW], v1r[:, 0:TW], AF.Ln))
        ac_ln = emit("scalar", lambda e: e.engine_nop() if False else e.activation(
            q1r[:, 0:TW], v1r[:, 0:TW], AF.Copy), s_ac)
        wait("vector", s_ac, ac_ln)
        dv_half = emit("vector", lambda e: e.tensor_scalar_mul(
            q1r[:, 0:TW], q1r[:, 0:TW], -0.5), s_dv)
        wait("scalar", s_dv, dv_half)
        ac_rs = emit("scalar", lambda e: e.activation(r1r[:, 0:TW], q1r[:, 0:TW],
                                                      AF.Exp), s_ac)
        wait("vector", s_ac, ac_rs)
        on("vector", lambda e: e.tensor_copy(mh1[:, 0:TW], m1r[:, 0:TW]))
        on("vector", lambda e: e.tensor_copy(rh1[:, 0:TW], r1r[:, 0:TW]))
        dv_sp = emit("vector", lambda e: e.tensor_sub(rl1[:, 0:TW], r1r[:, 0:TW],
                                                      rh1[:, 0:TW]), s_dv)
        wait("tensor", s_dv, dv_sp)
        if app_prev:
            wait("tensor", s_dv, app_prev)

        def bc(e):
            e.matmul(bank_b[:, 0:TW], ones_sb[32:33, :], mh1[:, 0:TW],
                     start=True, stop=True, tile_position=(32, 0))
            e.matmul(bank_b[:, TW:2 * TW], ones_sb[32:33, :], rh1[:, 0:TW],
                     start=True, stop=False, tile_position=(32, 0))
            return e.matmul(bank_b[:, TW:2 * TW], ones_sb[32:33, :], rl1[:, 0:TW],
                            start=False, stop=True, tile_position=(32, 0))
        pe_bc = emit("tensor", bc, s_pe)
        wait("vector", s_pe, pe_bc)
        for kc in range(8):
            on("vector", lambda e, k=kc: e.tensor_sub(xf1[:, k, :],
                                                      xsrc[:, k, :],
                                                      bank_b[:, 0:TW]))
        app = None
        if not split:
            for kc in range(8):
                app = emit("vector", lambda e, k=kc: e.tensor_mul(
                    out_dst(k), xf1[:, k, :], bank_b[:, TW:2 * TW]),
                    s_dv if kc == 7 else None)
        else:
            for kc in range(8):
                on("vector", lambda e, k=kc: e.tensor_mul(
                    xf1[:, k, :], xf1[:, k, :], bank_b[:, TW:2 * TW]))
            for kc in range(8):
                on("vector", lambda e, k=kc: e.tensor_copy(out_dst(k),
                                                           xf1[:, k, :]))
            for kc in range(8):
                app = emit("vector", lambda e, k=kc: e.tensor_sub(
                    xls[:, k, :], xf1[:, k, :], out_dst(k)),
                    s_dv if kc == 7 else None)
        return fin, app

    for t in range(NTILE):
        p = t % 2
        tok = slice(TW * t, TW * t + TW)
        if t >= 2:
            wait("sync", s_gd, mark_x2out[t - 2])
            wait("sync", s_dv, mark_appY[t - 2])
        ld_x = emit("sync", lambda e, pp=p, tk=tok: e.dma_start(
            out=xt[pp][:], in_=xT[:, tk].rearrange("(c p) t -> p c t", p=128)),
            s_ld, 16)
        wait("vector", s_ld, ld_x)
        if t == 0:
            wait("tensor", s_gd, gd_wu)
            wait("vector", s_gd, gd_mt)
        else:
            wait("vector", s_pe, mark_peEnd[t - 1])
        if t >= 1:
            wait("vector", s_sd, mark_xl1out[t - 1])
        fin, app = emit_ln(xt[p], LXs, LXb, lambda k: xbf[:, k, :],
                           mark_finX[t - 1] if t else 0,
                           mark_appX[t - 1] if t else 0, split=True)
        mark_finX[t], mark_appX[t] = fin, app
        wait("scalar", s_dv, app)
        emit("scalar", lambda e, tk=tok: e.dma_start(
            out=xl1h[:, tk].rearrange("(c p) t -> p c t", p=128), in_=xbf[:]),
            s_sd, 16)
        mark_xl1out[t] = emit("scalar", lambda e, tk=tok: e.dma_start(
            out=xl1l[:, tk].rearrange("(c p) t -> p c t", p=128), in_=xls[:]),
            s_sd, 16)

        wait("tensor", s_dv, app)
        stt = []
        for oc in range(8):
            y = Y1v[oc % 2]
            if oc >= 2:
                wait("tensor", s_dv, stt[oc - 2])

            def ffn1(e, o=oc, yy=y):
                last = None
                for kc in range(8):
                    last = e.matmul(yy, Mt[:, kc, 128 * o:128 * o + 128],
                                    xbf[:, kc, :], start=(kc == 0), stop=(kc == 7))
                return last
            pe_f = emit("tensor", ffn1, s_pe)
            wait("vector", s_pe, pe_f)
            stt.append(emit("vector", lambda e, o=oc, yy=y, pp=p: e.scalar_tensor_tensor(
                xt[pp][:, o, :], yy, fb2_sb[:, o:o + 1], xt[pp][:, o, :],
                OP.add, OP.add), s_dv))

        wait("gpsimd", s_dv, stt[-1])
        mark_x2out[t] = emit("gpsimd", lambda e, pp=p, tk=tok: e.dma_start(
            out=x2h[:, tk].rearrange("(c p) t -> p c t", p=128), in_=xt[pp][:]),
            s_gd, 16)

        wait("vector", s_sd, mark_kout[t - 1] if t else 0)
        wait("vector", s_pe, mark_peEnd[t - 1] if t else 0)
        fin2, app2 = emit_ln(xt[p], LYs, LYb, lambda k: kstg[:, k, :],
                             mark_finY[t - 1] if t else 0,
                             mark_appY[t - 1] if t else 0)
        mark_finY[t], mark_appY[t] = fin2, app2

        wait("scalar", s_dv, app2)
        for pr in range(NPAIR):
            mark_kout[t] = emit("scalar", lambda e, r=pr, tt=t: e.dma_start(
                out=keysh[r, tt], in_=kstg[:, r, :]), s_sd, 16)

        wait("tensor", s_dv, app2)
        if t:
            wait("vector", s_sd, mark_vout[t - 1])
            wait("tensor", s_dv, mark_vevac[t - 1])
        ev = []
        for kh in range(2):
            for g in range(2):
                vp = VPv[(2 * kh + g) % 2]
                if len(ev) >= 2:
                    wait("tensor", s_dv, ev[-2])

                def vproj(e, k=kh, gg=g, vv=vp):
                    last = None
                    for i in range(4):
                        pr = 4 * gg + i
                        last = e.matmul(vv[:, 128 * i:128 * i + 128],
                                        kstg[:, pr, 128 * k:128 * k + 128], Wv2[:],
                                        start=True, stop=True)
                    return last
                pe_v = emit("tensor", vproj, s_pe)
                wait("vector", s_pe, pe_v)
                for i4 in range(4):
                    mk_e = emit("vector", lambda e, k=kh, gg=g, i=i4, vv=vp:
                                e.tensor_copy(
                                    vstg[:, k, 4 * gg + i, :, 0:64],
                                    vv[:, 128 * i:128 * i + 128].rearrange(
                                        "p (h c) -> p h c", h=2)), s_dv)
                ev.append(mk_e)
        mark_vevac[t] = ev[-1]
        wait("scalar", s_dv, ev[-1])
        for pr in range(NPAIR):
            mark_vout[t] = emit("scalar", lambda e, r=pr, tt=t: e.dma_start(
                out=vh[r, 2 * tt:2 * tt + 2].rearrange("k p f -> p k f"),
                in_=vstg[:, :, r, :, :].rearrange("p k h c -> p k (h c)")), s_sd, 16)
        mark_peEnd[t] = s_pe.v

    import kernel as _km2
    if getattr(_km2, "DEBUG_DUMP", False):
        dx2 = nc.dram_tensor("dbg_x2h", [D, T], F32, kind="ExternalOutput").ap()
        dke = nc.dram_tensor("dbg_keysh", [NPAIR, NTP, 128, 256], BF16,
                             kind="ExternalOutput").ap()
        dvh = nc.dram_tensor("dbg_vh", [NPAIR, NKT, 128, 130], BF16,
                             kind="ExternalOutput").ap()
        wait("gpsimd", s_sd, s_sd.v)
        wait("gpsimd", s_gd, s_gd.v)
        emit("gpsimd", lambda e: e.dma_start(out=dx2, in_=x2h), s_gd, 16)
        emit("gpsimd", lambda e: e.dma_start(out=dke, in_=keysh), s_gd, 16)
        emit("gpsimd", lambda e: e.dma_start(out=dvh, in_=vh), s_gd, 16)
        dmm = nc.dram_tensor("dbg_mm", [D, D], F32, kind="ExternalOutput").ap()
        emit("gpsimd", lambda e: e.dma_start(out=dmm, in_=ccout.ap()), s_gd, 16)
        dci = nc.dram_tensor("dbg_ci", [128, D], F32, kind="ExternalOutput").ap()
        emit("gpsimd", lambda e: e.dma_start(out=dci, in_=ccin.ap()), s_gd, 16)
        dvs = nc.dram_tensor("dbg_vstg", [128, 2, 8, 2, 65], BF16,
                             kind="ExternalOutput").ap()
        emit("gpsimd", lambda e: e.dma_start(out=dvs, in_=vstg[:]), s_gd, 16)
        wait("gpsimd", s_gd, s_gd.v)
    BAR = dict(ld=s_ld.v, sd=s_sd.v, gd=s_gd.v, pe=s_pe.v, dv=s_dv.v, ac=s_ac.v)

    with nc.Block() as block:
        for ename, meth in (("sync", block.sync), ("tensor", block.tensor),
                            ("vector", block.vector), ("scalar", block.scalar),
                            ("gpsimd", block.gpsimd)):
            def runner(eng, _steps=plan[ename]):
                for f in _steps:
                    f(eng)
            meth(runner)

        S_dict = dict(Mt=Mt, G2=G2, Wv2=Wv2, tri_sb=tri_sb, eye_sb=eye_sb,
                      ones_sb=ones_sb, buT_sb=buT_sb, fb2_sb=fb2_sb,
                      m1r=m1r, v1r=v1r, q1r=q1r, r1r=r1r, mh1=mh1,
                      rh1=rh1, rl1=rl1, rzr=rzr, rzf=rzf, tmpB=tmpB,
                      xf1=xf1, kst=kst, vsw=vsw, Mt_l=Mt_l, xbl3=xbl3,
                      q_sb=q_sb, P_sb=P_sb, osb=osb, on_sb=on_sb, x3=x3,
                      xb3=xb3, x2r=x2r, o4=o4, wut=wut,
                      SC=SC, oAv=oAv, oBv=oBv, pjv=pjv, L3v=L3v)
        H_dict = dict(keysh=keysh, vh=vh, x2h=x2h, wuh=wuh, out=out,
                      xl1h=xl1h, xl1l=xl1l)
        sems_d = dict(s_ld=s_ld, s_sd=s_sd, s_gd=s_gd, s_pe=s_pe,
                      s_dv=s_dv, s_ac=s_ac)

        class _SkipInst:
            def then_inc(self, *a, **k):
                return self

        _SKIP = _SkipInst()

        class _EngProxy:
            def __init__(self, real, on):
                self._real = real
                self._on = on

            def __getattr__(self, name):
                if self._on:
                    return getattr(self._real, name)
                return lambda *a, **k: _SKIP

        def _one_engine_switch(eng):
            pid = nc.partition_id(engines=[eng.engine])
            reg = eng.compute_val(pid % 2)
            names = ("tensor", "vector", "scalar", "sync")
            reals = (nc.tensor, nc.vector, nc.scalar, nc.sync)
            for arm in nc.Switch(engines=[eng], index=[reg], n=2):
                proxies = [_EngProxy(r, r is eng) for r in reals]
                _emit_arm(nc, arm, proxies, BAR, sems_d, S_dict, H_dict)

        @block.tensor
        def _(tensor):
            _one_engine_switch(nc.tensor)

        @block.vector
        def _(vector):
            _one_engine_switch(nc.vector)

        @block.scalar
        def _(scalar):
            _one_engine_switch(nc.scalar)

        @block.sync
        def _(sync):
            _one_engine_switch(nc.sync)

    es.close()
    return nc


ARM_LVL = 9
UNIT_BARRIER = True


def _emit_arm(nc, arm, engines, BAR, sems, S, H):
    import kernel as _self
    LVL = getattr(_self, "ARM_LVL", 9)
    tensor, vector, scalar, sync = engines
    s_ld, s_sd, s_gd, s_pe, s_dv, s_ac = (sems["s_ld"], sems["s_sd"],
                                          sems["s_gd"], sems["s_pe"],
                                          sems["s_dv"], sems["s_ac"])
    ld = BAR["ld"]; sd = BAR["sd"]; gd = BAR["gd"]
    pe = BAR["pe"]; dv = BAR["dv"]; ac = BAR["ac"]

    Mt, G2, Wv2 = S["Mt"], S["G2"], S["Wv2"]
    tri_sb, ones_sb = S["tri_sb"], S["ones_sb"]
    buT_sb, fb2_sb = S["buT_sb"], S["fb2_sb"]
    m1r, v1r, q1r, r1r = S["m1r"], S["v1r"], S["q1r"], S["r1r"]
    mh1, rh1, rl1, rzr = S["mh1"], S["rh1"], S["rl1"], S["rzr"]
    rzf = S["rzf"]
    tmpB = S["tmpB"]
    xf1 = S["xf1"]
    Mt_l = S["Mt_l"]
    xbl3 = S["xbl3"]
    kst, vsw, q_sb, P_sb = S["kst"], S["vsw"], S["q_sb"], S["P_sb"]
    osb, on_sb, x3, xb3 = S["osb"], S["on_sb"], S["x3"], S["xb3"]
    x2r, o4, wut = S["x2r"], S["o4"], S["wut"]
    SC, oAv, oBv, pjv, L3v = S["SC"], S["oAv"], S["oBv"], S["pjv"], S["L3v"]
    keysh, vh, x2h, wuh, out = H["keysh"], H["vh"], H["x2h"], H["wuh"], H["out"]
    xl1h, xl1l = H["xl1h"], H["xl1l"]

    for e in engines:
        e.wait_ge(s_ld.h, BAR["ld"])
        e.wait_ge(s_sd.h, BAR["sd"])
        e.wait_ge(s_gd.h, BAR["gd"])
        e.wait_ge(s_pe.h, BAR["pe"])
        e.wait_ge(s_dv.h, BAR["dv"])
        e.wait_ge(s_ac.h, BAR["ac"])

    units = UNITS[arm]
    m_kst_free_pe = 0
    m_vsw_free_pe = 0
    m_sc_exp = {}
    m_sc_dvq = 0
    m_oA_dv = 0
    m_on_pe = 0
    m_x2r_dv = [0, 0]
    m_tmpB_ld = [0, 0]
    m_xb3_free_pe = 0
    m_o4_gd = [0, 0]
    m_o4_dv = [0, 0]
    m_wut_pe = [0, 0]
    m_pj_dv = 0
    m_L3_dv = 0
    m_x3_dv = 0
    exp_done_prev_pair = 0

    UB = getattr(_self, "UNIT_BARRIER", False)
    for ui, u in enumerate(units):
        if UB and ui:
            for e in (tensor, vector, scalar, sync):
                e.wait_ge(s_pe.h, pe)
                e.wait_ge(s_dv.h, dv)
                e.wait_ge(s_ac.h, ac)
                e.wait_ge(s_ld.h, ld)
                e.wait_ge(s_sd.h, sd)
        nkt = 4 * u + 4
        ntp = 2 * u + 2
        slot = ui
        for pr in range(NPAIR):
            if m_kst_free_pe:
                sync.wait_ge(s_pe.h, m_kst_free_pe)
            ld += 16
            sync.dma_start(out=kst[:, 0:ntp, :],
                           in_=keysh[pr, 0:ntp].rearrange("k p t -> p k t")
                           ).then_inc(s_ld.h, 16)
            ld_k = ld
            if m_vsw_free_pe:
                sync.wait_ge(s_pe.h, m_vsw_free_pe)
            ld += 16
            sync.dma_start(out=vsw[:, 0:nkt, :, :].rearrange("p k h c -> p k (h c)"),
                           in_=vh[pr, 0:nkt].rearrange("k p f -> p k f")
                           ).then_inc(s_ld.h, 16)
            ld_v = ld

            if LVL < 1:
                continue
            tensor.wait_ge(s_ld.h, ld_k)
            if m_sc_exp.get(0):
                tensor.wait_ge(s_ac.h, m_sc_exp[0])
            if m_sc_dvq:
                tensor.wait_ge(s_dv.h, m_sc_dvq)
            for half in range(2):
                pe += 1
                tensor.matmul(SC[0][:, 512 * 0 + 256 * half:256 * half + 256 + 0],
                              G2[:], kst[:, 2 * u + half, :],
                              start=True, stop=True).then_inc(s_pe.h, 1)
            vector.wait_ge(s_pe.h, pe)
            dv += 1
            vector.tensor_copy(q_sb[:], SC[0][:, 0:512]).then_inc(s_dv.h, 1)
            m_sc_dvq = dv
            q_ready_dv = dv

            for kt in range(nkt if LVL >= 2 else 0):
                d = kt - 4 * u
                qoff = 128 * d if d > 0 else 0
                npart = 512 - qoff
                sc = SC[kt % 2]
                scm_prev = m_sc_exp.get(kt % 2, 0)
                tensor.wait_ge(s_dv.h, q_ready_dv)
                if scm_prev:
                    tensor.wait_ge(s_ac.h, scm_prev)
                pe += 1
                tensor.matmul(sc[0:128, qoff:512],
                              kst[0:64, kt // 2, 128 * (kt % 2):128 * (kt % 2) + 128],
                              q_sb[0:64, qoff:512], start=True, stop=True,
                              tile_position=(0, 0)).then_inc(s_pe.h, 1)
                pe += 1
                tensor.matmul(sc[0:128, 512 + qoff:1024],
                              kst[64:128, kt // 2, 128 * (kt % 2):128 * (kt % 2) + 128],
                              q_sb[64:128, qoff:512], start=True, stop=True,
                              tile_position=(64, 0)).then_inc(s_pe.h, 1)
                pe_sc = pe
                mask_dv = 0
                if d >= 0 and LVL >= 3:
                    vector.wait_ge(s_pe.h, pe_sc)
                    dv += 1
                    vector.tensor_add(
                        sc[:, qoff:qoff + 128].rearrange("p (s t) -> p s t", s=1),
                        sc[:, qoff:qoff + 128].rearrange("p (s t) -> p s t", s=1),
                        tri_sb[:].rearrange("p (s t) -> p s t", s=1)
                    ).then_inc(s_dv.h, 1)
                    dv += 1
                    vector.tensor_add(
                        sc[:, 512 + qoff:512 + qoff + 128],
                        sc[:, 512 + qoff:512 + qoff + 128],
                        tri_sb[:]).then_inc(s_dv.h, 1)
                    mask_dv = dv
                if LVL < 4:
                    continue
                scalar.wait_ge(s_pe.h, pe_sc)
                if mask_dv:
                    scalar.wait_ge(s_dv.h, mask_dv)
                ac += 1
                scalar.activation(
                    P_sb[kt % 2][:].rearrange("p (s q) -> p s q", s=2)[:, :, qoff:512],
                    sc.rearrange("p (s q) -> p s q", s=2)[:, :, qoff:512],
                    AF.Exp).then_inc(s_ac.h, 1)
                m_sc_exp[kt % 2] = ac
                ac_exp = ac
                if LVL < 5:
                    continue
                tensor.wait_ge(s_ac.h, ac_exp)
                tensor.wait_ge(s_ld.h, ld_v)
                if kt == 0 and m_oA_dv:
                    tensor.wait_ge(s_dv.h, m_oA_dv)
                pe += 1
                tensor.matmul(oAv[:, qoff:512], vsw[:, kt, 0, :],
                              P_sb[kt % 2][:, qoff:512], start=(kt == 0),
                              stop=(kt == nkt - 1)).then_inc(s_pe.h, 1)
                pe += 1
                tensor.matmul(oBv[:, qoff:512], vsw[:, kt, 1, :],
                              P_sb[kt % 2][:, 512 + qoff:1024], start=(kt == 0),
                              stop=(kt == nkt - 1)).then_inc(s_pe.h, 1)
            m_kst_free_pe = pe
            m_vsw_free_pe = pe
            pe_av_done = pe

            if LVL < 6:
                continue
            vector.wait_ge(s_pe.h, pe_av_done)
            vector.reciprocal(rzf[:, 0:512], oAv[64:65, :])
            vector.reciprocal(rzf[:, 512:1024], oBv[64:65, :])
            dv += 1
            vector.tensor_copy(rzr[:], rzf[:]).then_inc(s_dv.h, 1)
            rz_dv = dv
            dv += 1
            vector.tensor_copy(osb[0:64, 0:512], oAv[0:64, :]).then_inc(s_dv.h, 1)
            dv += 1
            vector.tensor_copy(osb[0:64, 512:1024], oBv[0:64, :]).then_inc(s_dv.h, 1)
            m_oA_dv = dv
            tensor.wait_ge(s_dv.h, rz_dv)
            if m_sc_exp.get(0):
                tensor.wait_ge(s_ac.h, m_sc_exp[0])
            if m_sc_exp.get(1):
                tensor.wait_ge(s_ac.h, m_sc_exp[1])
            if m_sc_dvq:
                tensor.wait_ge(s_dv.h, m_sc_dvq)
            pe += 1
            tensor.matmul(SC[0][0:64, 0:512], ones_sb[64:65, 0:64], rzr[:, 0:512],
                          start=True, stop=True,
                          tile_position=(64, 0)).then_inc(s_pe.h, 1)
            pe += 1
            tensor.matmul(SC[0][0:64, 512:1024], ones_sb[64:65, 0:64],
                          rzr[:, 512:1024], start=True, stop=True,
                          tile_position=(64, 0)).then_inc(s_pe.h, 1)
            pe_rzb = pe
            vector.wait_ge(s_pe.h, pe_rzb)
            vector.wait_ge(s_dv.h, m_oA_dv)
            if m_on_pe:
                vector.wait_ge(s_pe.h, m_on_pe)
            if m_tmpB_ld[pr % 2]:
                vector.wait_ge(s_ld.h, m_tmpB_ld[pr % 2])
            dv += 1
            vector.tensor_mul(on_sb[0:64, pr, :], osb[0:64, 0:512],
                              SC[0][0:64, 0:512]).then_inc(s_dv.h, 1)
            dv += 1
            vector.tensor_mul(tmpB[pr % 2][:], osb[0:64, 512:1024],
                              SC[0][0:64, 512:1024]).then_inc(s_dv.h, 1)
            m_sc_dvq = dv
            sync.wait_ge(s_dv.h, dv)
            ld += 16
            sync.dma_start(out=on_sb[64:128, pr, :],
                           in_=tmpB[pr % 2][:]).then_inc(s_ld.h, 16)
            m_tmpB_ld[pr % 2] = ld
            on_ready_dv = dv
            on_ready_ld = ld
            exp_done_prev_pair = ac

        if LVL < 7:
            continue
        if m_xb3_free_pe:
            sync.wait_ge(s_pe.h, m_xb3_free_pe)
        ld += 16
        sync.dma_start(out=xb3[:], in_=xl1h[:, 512 * u:512 * u + 512].rearrange(
            "(c p) t -> p c t", p=128)).then_inc(s_ld.h, 16)
        ld += 16
        sync.dma_start(out=xbl3[:], in_=xl1l[:, 512 * u:512 * u + 512].rearrange(
            "(c p) t -> p c t", p=128)).then_inc(s_ld.h, 16)
        ld_xl1 = ld
        grp_dv = []
        grp_pe = []
        for oc in range(8):
            par = oc % 2
            if m_wut_pe[par]:
                sync.wait_ge(s_pe.h, m_wut_pe[par])
            ld += 16
            sync.dma_start(out=wut[par][:], in_=wuh[oc]).then_inc(s_ld.h, 16)
            ld_wu = ld
            if m_x2r_dv[par]:
                sync.wait_ge(s_dv.h, m_x2r_dv[par])
            ld += 16
            sync.dma_start(out=x2r[par][:],
                           in_=x2h[:, 512 * u:512 * u + 512].rearrange(
                               "(c p) t -> p c t", p=128)[:, oc, :]).then_inc(s_ld.h, 16)
            ld_x2 = ld

            bank = pjv if oc % 2 == 0 else L3v
            tensor.wait_ge(s_ld.h, max(ld_wu, on_ready_ld, ld_xl1))
            tensor.wait_ge(s_dv.h, on_ready_dv)
            if len(grp_dv) >= 2:
                tensor.wait_ge(s_dv.h, grp_dv[oc - 2])
            if oc == 0 and m_pj_dv:
                tensor.wait_ge(s_dv.h, m_pj_dv)
            if oc == 1 and m_L3_dv:
                tensor.wait_ge(s_dv.h, m_L3_dv)
            ocs = slice(128 * oc, 128 * oc + 128)
            for pr in range(NPAIR):
                tensor.matmul(bank, wut[par][:, pr, :], on_sb[:, pr, :],
                              start=(pr == 0), stop=False)
            for kc in range(8):
                tensor.matmul(bank, Mt_l[:, kc, ocs], xb3[:, kc, :],
                              start=False, stop=False)
            last = None
            for kc in range(8):
                last = tensor.matmul(bank, Mt[:, kc, ocs], xbl3[:, kc, :],
                                     start=False, stop=(kc == 7))
            pe += 1
            last.then_inc(s_pe.h, 1)
            grp_pe.append(pe)
            m_wut_pe[par] = pe
            vector.wait_ge(s_pe.h, pe)
            vector.wait_ge(s_ld.h, ld_x2)
            dv += 1
            vector.scalar_tensor_tensor(x3[:, oc, :], bank, buT_sb[:, oc:oc + 1],
                                        x2r[par][:], OP.add,
                                        OP.add).then_inc(s_dv.h, 1)
            grp_dv.append(dv)
            m_x2r_dv[par] = dv
        m_on_pe = grp_pe[-1]
        x3_dv = grp_dv[-1]
        m_pj_dv = grp_dv[-2]
        m_L3_dv = grp_dv[-1]

        if LVL < 8:
            continue
        W3 = 512
        vector.wait_ge(s_dv.h, x3_dv)
        dv += 1
        vector.tensor_copy(xb3[:].rearrange("p c t -> p (c t)"),
                           x3[:].rearrange("p c t -> p (c t)")).then_inc(s_dv.h, 1)
        cast3 = dv
        tensor.wait_ge(s_dv.h, cast3)
        tensor.wait_ge(s_dv.h, grp_dv[-2])

        def chain3(bank):
            last = None
            for kc in range(8):
                last = tensor.matmul(bank[32:33, 0:W3], ones_sb[:, 0:1],
                                     xb3[:, kc, :], start=(kc == 0), stop=(kc == 7),
                                     tile_position=(0, 32))
            return last
        tensor.wait_ge(s_dv.h, grp_dv[-1])
        pe += 1
        chain3(pjv).then_inc(s_pe.h, 1)
        vector.wait_ge(s_pe.h, pe)
        dv += 1
        vector.tensor_mul(xb3[:].rearrange("p c t -> p (c t)"),
                          xb3[:].rearrange("p c t -> p (c t)"),
                          xb3[:].rearrange("p c t -> p (c t)")).then_inc(s_dv.h, 1)
        tensor.wait_ge(s_dv.h, dv)
        pe += 1
        chain3(L3v).then_inc(s_pe.h, 1)
        vector.wait_ge(s_pe.h, pe)
        vector.tensor_scalar_mul(m1r[:, 0:W3], pjv[32:33, 0:W3], RD)
        vector.tensor_scalar_mul(v1r[:, 0:W3], L3v[32:33, 0:W3], RD)
        vector.tensor_mul(q1r[:, 0:W3], m1r[:, 0:W3], m1r[:, 0:W3])
        dv += 1
        vector.tensor_sub(v1r[:, 0:W3], v1r[:, 0:W3], q1r[:, 0:W3]).then_inc(s_dv.h, 1)
        vector.tensor_scalar_add(v1r[:, 0:W3], v1r[:, 0:W3], EPS)
        scalar.wait_ge(s_dv.h, dv)
        scalar.activation(v1r[:, 0:W3], v1r[:, 0:W3], AF.Ln)
        ac += 1
        scalar.activation(q1r[:, 0:W3], v1r[:, 0:W3], AF.Copy).then_inc(s_ac.h, 1)
        vector.wait_ge(s_ac.h, ac)
        dv += 1
        vector.tensor_scalar_mul(q1r[:, 0:W3], q1r[:, 0:W3],
                                 -0.5).then_inc(s_dv.h, 1)
        scalar.wait_ge(s_dv.h, dv)
        ac += 1
        scalar.activation(r1r[:, 0:W3], q1r[:, 0:W3], AF.Exp).then_inc(s_ac.h, 1)
        vector.wait_ge(s_ac.h, ac)
        vector.tensor_copy(mh1[:, 0:W3], m1r[:, 0:W3])
        vector.tensor_copy(rh1[:, 0:W3], r1r[:, 0:W3])
        dv += 1
        vector.tensor_sub(rl1[:, 0:W3], r1r[:, 0:W3], rh1[:, 0:W3]).then_inc(s_dv.h, 1)
        tensor.wait_ge(s_dv.h, dv)
        tensor.matmul(L3v[:, 0:W3], ones_sb[32:33, :], mh1[:, 0:W3],
                      start=True, stop=True, tile_position=(32, 0))
        tensor.matmul(pjv[:, 0:W3], ones_sb[32:33, :], rh1[:, 0:W3],
                      start=True, stop=False, tile_position=(32, 0))
        pe += 1
        tensor.matmul(pjv[:, 0:W3], ones_sb[32:33, :], rl1[:, 0:W3],
                      start=False, stop=True,
                      tile_position=(32, 0)).then_inc(s_pe.h, 1)
        bc3 = pe
        vector.wait_ge(s_pe.h, bc3)
        dv += 1
        last = None
        for hw in range(2):
            hsl = slice(256 * hw, 256 * hw + 256)
            for kc in range(8):
                vector.tensor_sub(xf1[:, kc, :], x3[:, kc, hsl], L3v[:, hsl])
                vector.tensor_mul(xf1[:, kc, :], xf1[:, kc, :], pjv[:, hsl])
                vector.tensor_copy(xb3[:, kc, hsl], xf1[:, kc, :])
                last = vector.tensor_sub(xbl3[:, kc, hsl], xf1[:, kc, :],
                                         xb3[:, kc, hsl])
        last.then_inc(s_dv.h, 1)
        app3 = dv
        m_pj_dv = app3
        m_L3_dv = app3

        if LVL < 9:
            continue
        tensor.wait_ge(s_dv.h, app3)
        f2dv = []
        for oc in range(8):
            bank = pjv if oc % 2 == 0 else L3v
            par = oc % 2
            if len(f2dv) >= 2:
                tensor.wait_ge(s_dv.h, f2dv[oc - 2])
            ocs2 = slice(128 * oc, 128 * oc + 128)
            for kc in range(8):
                tensor.matmul(bank, Mt[:, kc, ocs2], xb3[:, kc, :],
                              start=(kc == 0), stop=False)
            for kc in range(8):
                tensor.matmul(bank, Mt_l[:, kc, ocs2], xb3[:, kc, :],
                              start=False, stop=False)
            last = None
            for kc in range(8):
                last = tensor.matmul(bank, Mt[:, kc, ocs2], xbl3[:, kc, :],
                                     start=False, stop=(kc == 7))
            pe += 1
            last.then_inc(s_pe.h, 1)
            vector.wait_ge(s_pe.h, pe)
            if m_o4_gd[par]:
                vector.wait_ge(s_sd.h, m_o4_gd[par])
            dv += 1
            vector.scalar_tensor_tensor(o4[par][:], bank, fb2_sb[:, oc:oc + 1],
                                        x3[:, oc, :], OP.add,
                                        OP.add).then_inc(s_dv.h, 1)
            f2dv.append(dv)
            m_o4_dv[par] = dv
            scalar.wait_ge(s_dv.h, dv)
            sd += 16
            scalar.dma_start(
                out=out[:, 512 * slot:512 * slot + 512].rearrange(
                    "(c p) t -> p c t", p=128)[:, oc, :],
                in_=o4[par][:]).then_inc(s_sd.h, 16)
            m_o4_gd[par] = sd
        m_pj_dv = f2dv[-2]
        m_L3_dv = f2dv[-1]
        m_x3_dv = f2dv[-1]
        m_xb3_free_pe = pe

    scalar.wait_ge(s_sd.h, sd)


_NC_CACHE = None


def _get_nc():
    global _NC_CACHE
    if _NC_CACHE is None:
        _NC_CACHE = _build()
    return _NC_CACHE


def kernel(L0_attn_x, L0_attn_Wq, L0_attn_Wk, L0_attn_Wv, L0_attn_Wu, L0_attn_bu,
           L0_ffn_w_freq_real, L0_ffn_w_freq_imag, L0_ffn_bias):
    x = np.asarray(L0_attn_x, np.float32)
    Wq = np.ascontiguousarray(np.asarray(L0_attn_Wq, np.float32))
    Wk = np.ascontiguousarray(np.asarray(L0_attn_Wk, np.float32))
    WvT_ = np.ascontiguousarray(np.asarray(L0_attn_Wv, np.float32).T)
    WuT_ = np.ascontiguousarray(np.asarray(L0_attn_Wu, np.float32).T)
    bu = np.asarray(L0_attn_bu, np.float32)
    wfr_ = np.ascontiguousarray(np.asarray(L0_ffn_w_freq_real, np.float32))
    wfi_ = np.ascontiguousarray(np.asarray(L0_ffn_w_freq_imag, np.float32))
    fb = np.asarray(L0_ffn_bias, np.float32)

    bfd = ml_dtypes.bfloat16

    def split_hl(a32):
        h = a32.astype(bfd)
        lo = (a32 - h.astype(np.float32)).astype(bfd)
        return h, lo

    idx = np.arange(D, dtype=np.int64)
    ang = (2.0 * np.pi / D) * ((idx[:, None] * idx[None, :]) % D).astype(np.float64)
    C = np.cos(ang)
    Sn = np.sin(ang)
    c2_h, c2_l = split_hl((C * (0.5 / D)).astype(np.float32))
    s2_h, s2_l = split_hl((-Sn * (0.5 / D)).astype(np.float32))
    c2s_ = np.ascontiguousarray(np.stack([c2_h, c2_l, s2_h, s2_l]))
    tri_ = np.where(np.arange(128)[:, None] > np.arange(128)[None, :],
                    np.float32(-1e30), np.float32(0.0)).astype(np.float32)
    buT_ = np.ascontiguousarray(bu.reshape(8, 128).T)
    fbT_ = np.ascontiguousarray(fb.reshape(8, 128).T)

    in_maps = []
    for c in range(8):
        b = c >> 1
        rows = slice(128 * c, 128 * c + 128)
        ch, cl = split_hl(C[:, rows].astype(np.float32))
        sh, sl = split_hl(Sn[:, rows].astype(np.float32))
        nh, nl = split_hl((-Sn[:, rows]).astype(np.float32))
        in_maps.append({
            "xT": np.ascontiguousarray(x[b].T),
            "wfr": wfr_, "wfi": wfi_,
            "c1s": np.ascontiguousarray(np.stack([ch, cl, sh, sl, nh, nl])),
            "c2s": c2s_,
            "wq": Wq, "wk": Wk, "wvT": WvT_, "WuT": WuT_,
            "buT": buT_, "fbT": fbT_, "tri": tri_,
        })

    nc = _get_nc()
    res = run_bass_kernel_spmd(nc, in_maps, core_ids=list(range(8)))

    y = np.empty((B, T, D), np.float32)
    for b in range(B):
        outA = res.results[2 * b]["out"]
        outB = res.results[2 * b + 1]["out"]
        yT = np.empty((D, T), np.float32)
        for slot, u in enumerate(UNITS[0]):
            yT[:, 512 * u:512 * u + 512] = outA[:, 512 * slot:512 * slot + 512]
        for slot, u in enumerate(UNITS[1]):
            yT[:, 512 * u:512 * u + 512] = outB[:, 512 * slot:512 * slot + 512]
        y[b] = yT.T
    return y


# version 07 (final)
# speedup vs baseline: 1.3077x; 1.3077x over previous
import numpy as np
import ml_dtypes
from contextlib import ExitStack

import concourse.bass as bass
import concourse.mybir as mybir
from concourse.bass_utils import run_bass_kernel_spmd

F32 = mybir.dt.float32
BF16 = mybir.dt.bfloat16
AF = mybir.ActivationFunctionType
OP = mybir.AluOpType

D = 1024
T = 4096
B = 4
NPAIR = 8
NKT = 32
NTP = 16
NTILE = 16
TW = 256
UW = 512
EPS = 1e-5
RD = 1.0 / 1024.0
UNITS = [[0, 1, 6, 7], [2, 3, 4, 5]]


def _build(detect_races=True):
    nc = bass.Bass("TRN2", target_bir_lowering=False,
                   detect_race_conditions=detect_races)

    xT = nc.dram_tensor("xT", [D, T], F32, kind="ExternalInput").ap()
    wfr = nc.dram_tensor("wfr", [D, D], F32, kind="ExternalInput").ap()
    wfi = nc.dram_tensor("wfi", [D, D], F32, kind="ExternalInput").ap()
    c1s = nc.dram_tensor("c1s", [6, D, 128], BF16, kind="ExternalInput").ap()
    c2s = nc.dram_tensor("c2s", [4, D, D], BF16, kind="ExternalInput").ap()
    wq = nc.dram_tensor("wq", [64, 64], F32, kind="ExternalInput").ap()
    wk = nc.dram_tensor("wk", [64, 64], F32, kind="ExternalInput").ap()
    wvT = nc.dram_tensor("wvT", [64, 64], F32, kind="ExternalInput").ap()
    WuT = nc.dram_tensor("WuT", [D, D], F32, kind="ExternalInput").ap()
    buT = nc.dram_tensor("buT", [128, 8], F32, kind="ExternalInput").ap()
    fbT = nc.dram_tensor("fbT", [128, 8], F32, kind="ExternalInput").ap()
    tri = nc.dram_tensor("tri", [128, 128], F32, kind="ExternalInput").ap()
    out = nc.dram_tensor("out", [D, 2048], F32, kind="ExternalOutput").ap()

    x2h = nc.dram_tensor("x2h", [D, T], F32, kind="Internal").ap()
    xl1h = nc.dram_tensor("xl1h", [D, T], BF16, kind="Internal").ap()
    xl1l = nc.dram_tensor("xl1l", [D, T], BF16, kind="Internal").ap()
    keysh = nc.dram_tensor("keysh", [NPAIR, NTP, 128, 256], BF16, kind="Internal").ap()
    vh = nc.dram_tensor("vh", [NPAIR, NKT, 128, 130], BF16, kind="Internal").ap()
    wuh = nc.dram_tensor("wuh", [8, 128, 8, 128], BF16, kind="Internal").ap()
    ccin = nc.dram_tensor("ccin", [128, D], F32, kind="Internal")
    ccout = nc.dram_tensor("ccout", [D, D], F32, kind="Internal")

    eye_in = nc.inline_tensor(np.eye(128, dtype=ml_dtypes.bfloat16), name="eyebf").ap()
    ones_in = nc.inline_tensor(np.ones((128, 128), dtype=ml_dtypes.bfloat16),
                               name="onesbf").ap()

    es = ExitStack()

    def sb(name, shape, dt):
        return es.enter_context(nc.sbuf_tensor(name, shape, dt))

    G2 = sb("G2", [128, 128], BF16)
    Wv2 = sb("Wv2", [128, 128], BF16)
    eye_sb = sb("eye_sb", [128, 128], BF16)
    ones_sb = sb("ones_sb", [128, 128], BF16)
    tri_sb = sb("tri_sb", [128, 128], F32)
    buT_sb = sb("buT_sb", [128, 8], F32)
    fb2_sb = sb("fb2_sb", [128, 8], F32)
    _m1r = sb("m1r", [33, 512], F32)
    _v1r = sb("v1r", [33, 512], F32)
    _q1r = sb("q1r", [33, 512], F32)
    _r1r = sb("r1r", [33, 512], F32)
    _mh1 = sb("mh1", [33, 512], BF16)
    _rh1 = sb("rh1", [33, 512], BF16)
    _rl1 = sb("rl1", [33, 512], BF16)
    m1r, v1r, q1r, r1r = _m1r[32:33, :], _v1r[32:33, :], _q1r[32:33, :], _r1r[32:33, :]
    mh1, rh1, rl1 = _mh1[32:33, :], _rh1[32:33, :], _rl1[32:33, :]
    _rzr = sb("rzr", [65, 1024], BF16)
    _rzf = sb("rzf", [65, 1024], F32)
    rzr, rzf = _rzr[64:65, :], _rzf[64:65, :]
    xt = [sb("xt0", [128, 8, TW], F32), sb("xt1", [128, 8, TW], F32)]
    xbf = sb("xbf", [128, 8, TW], BF16)
    xls = sb("xls", [128, 8, TW], BF16)
    xf1 = sb("xf1", [128, 8, TW], F32)


    prep_psum = ExitStack()
    Aps = prep_psum.enter_context(nc.psum_tensor("Aps", [128, 1024], F32))
    Bps = prep_psum.enter_context(nc.psum_tensor("Bps", [128, 1024], F32))
    Mps = prep_psum.enter_context(nc.psum_tensor("Mps", [128, 1024], F32))
    tpa = prep_psum.enter_context(nc.psum_tensor("tpa", [128, 128], BF16))
    tpb = prep_psum.enter_context(nc.psum_tensor("tpb", [128, 128], BF16))

    class Sem:
        def __init__(self, name):
            self.h = es.enter_context(nc.semaphore(name))
            self.v = 0

    s_ld = Sem("s_ld")
    s_sd = Sem("s_sd")
    s_gd = Sem("s_gd")
    s_cc = Sem("s_cc")
    s_pe = Sem("s_pe")
    s_dv = Sem("s_dv")
    s_ac = Sem("s_ac")

    plan = {e: [] for e in ("sync", "tensor", "vector", "scalar", "gpsimd")}

    def on(e, fn):
        plan[e].append(fn)

    def wait(e, sem, val):
        if val and val > 0:
            plan[e].append(lambda eng, s=sem.h, v=val: eng.wait_ge(s, v))

    def emit(e, fn, sem=None, k=1):
        if sem is None:
            plan[e].append(fn)
            return None
        sem.v += k
        val = sem.v
        plan[e].append(lambda eng, f=fn, s=sem.h, kk=k: f(eng).then_inc(s, kk))
        return val

    prep_sbuf = ExitStack()
    whl = prep_sbuf.enter_context(nc.sbuf_tensor("whl", [128, 4, 8, 1024], BF16))
    crts = prep_sbuf.enter_context(nc.sbuf_tensor("crts", [128, 6, 8, 128], BF16))
    ABhl = prep_sbuf.enter_context(nc.sbuf_tensor("ABhl", [128, 4, 1024], BF16))
    ABthl = prep_sbuf.enter_context(nc.sbuf_tensor("ABthl", [128, 4, 1024], BF16))
    wsm = prep_sbuf.enter_context(nc.sbuf_tensor("wsm", [64, 3, 64], BF16))
    msl = prep_sbuf.enter_context(nc.sbuf_tensor("msl", [128, 1024], F32))

    for dst, s_ in ((tri_sb[:], tri), (buT_sb[:], buT), (fb2_sb[:], fbT),
                    (eye_sb[:], eye_in), (ones_sb[:], ones_in)):
        emit("sync", lambda e, d=dst, s=s_: e.dma_start(out=d, in_=s), s_ld, 16)
    ld_small = s_ld.v

    for dst, s_ in (
        (crts[:], c1s.rearrange("s (c p) m -> p s c m", p=128)),
        (wsm[:, 0, :], wq), (wsm[:, 1, :], wk), (wsm[:, 2, :], wvT),
    ):
        emit("gpsimd", lambda e, d=dst, s=s_: e.dma_start(out=d, in_=s), s_gd, 16)
    gd_w = s_gd.v

    with nc.sbuf_tensor("wf32", [128, 8, 1024], F32) as wf32:
        dv_wsp = 0
        for wi, wsrc in ((0, wfr), (1, wfi)):
            if dv_wsp:
                wait("sync", s_dv, dv_wsp)
            ldw = emit("sync", lambda e, s=wsrc: e.dma_start(
                out=wf32[:], in_=s.rearrange("(c p) n -> p c n", p=128)), s_ld, 16)
            wait("vector", s_ld, ldw)
            on("vector", lambda e, w=wi: e.tensor_copy(
                whl[:, 2 * w, :, :].rearrange("p c n -> p (c n)"),
                wf32[:].rearrange("p c n -> p (c n)")))
            dv_wsp = emit("vector", lambda e, w=wi: e.tensor_sub(
                whl[:, 2 * w + 1, :, :].rearrange("p c n -> p (c n)"),
                wf32[:].rearrange("p c n -> p (c n)"),
                whl[:, 2 * w, :, :].rearrange("p c n -> p (c n)")), s_dv)

    wait("tensor", s_gd, gd_w)
    pe_g = emit("tensor", lambda e: e.matmul(Mps[0:64, 0:64], wsm[:, 0, :],
                                             wsm[:, 1, :], start=True, stop=True), s_pe)
    wait("vector", s_gd, gd_w)
    wait("vector", s_ld, ld_small)
    on("vector", lambda e: e.memset(G2[:], 0.0))
    on("vector", lambda e: e.memset(Wv2[:], 0.0))
    wait("vector", s_pe, pe_g)
    on("vector", lambda e: e.tensor_scalar_mul(G2[0:64, 0:64], Mps[0:64, 0:64],
                                               1.0 / 32.0))
    on("vector", lambda e: e.tensor_copy(Wv2[0:64, 0:64], wsm[:, 2, :]))
    on("vector", lambda e: e.tensor_scalar_mul(fb2_sb[:], fb2_sb[:], 0.5))
    dv_g2 = emit("vector", lambda e: e.engine_nop(), s_dv)
    wait("sync", s_dv, dv_g2)
    emit("sync", lambda e: e.dma_start(out=G2[64:128, 64:128],
                                       in_=G2[0:64, 0:64]), s_ld, 16)
    gdiag_ld = emit("sync", lambda e: e.dma_start(out=Wv2[64:128, 64:128],
                                                  in_=Wv2[0:64, 0:64]), s_ld, 16)
    wait("tensor", s_ld, gdiag_ld)

    wait("tensor", s_dv, dv_wsp)

    def st1(e):
        last = None
        for dst_ps, terms in (
            (Aps, ((0, 0), (0, 1), (1, 0), (2, 2), (2, 3), (3, 2))),
            (Bps, ((0, 2), (0, 3), (1, 2), (4, 0), (4, 1), (5, 0))),
        ):
            for n2 in range(2):
                nsl = slice(512 * n2, 512 * n2 + 512)
                nterm = len(terms)
                for ti, (ci, wi) in enumerate(terms):
                    for kc in range(8):
                        last = e.matmul(dst_ps[:, nsl], crts[:, ci, kc, :],
                                        whl[:, wi, kc, nsl],
                                        start=(ti == 0 and kc == 0),
                                        stop=(ti == nterm - 1 and kc == 7))
        return last
    pe_st1 = emit("tensor", st1, s_pe)

    wait("vector", s_pe, pe_st1)
    on("vector", lambda e: e.tensor_copy(ABhl[:, 0, :], Aps[:]))
    on("vector", lambda e: e.tensor_sub(ABhl[:, 1, :], Aps[:], ABhl[:, 0, :]))
    on("vector", lambda e: e.tensor_copy(ABhl[:, 2, :], Bps[:]))
    dv_ab = emit("vector", lambda e: e.tensor_sub(ABhl[:, 3, :], Bps[:],
                                                  ABhl[:, 2, :]), s_dv)

    wait("tensor", s_dv, dv_ab)
    tp_marks = []
    for i in range(32):
        m, kc = divmod(i, 8)
        tp = tpa if i % 2 == 0 else tpb
        if i >= 2:
            wait("tensor", s_dv, tp_marks[i - 2])
        pe_t = emit("tensor", lambda e, t=tp, mm=m, k=kc: e.transpose(
            t[:], ABhl[:, mm, 128 * k:128 * k + 128], eye_sb[:]), s_pe)
        wait("vector", s_pe, pe_t)
        dvm = emit("vector", lambda e, t=tp, mm=m, k=kc: e.tensor_copy(
            ABthl[:, mm, 128 * k:128 * k + 128], t[:]), s_dv)
        tp_marks.append(dvm)
    dv_tp = s_dv.v

    import kernel as _kmA
    if getattr(_kmA, "DEBUG_DUMP", False):
        dab = nc.dram_tensor("dbg_ab", [128, 4, 1024], BF16, kind="ExternalOutput").ap()
        wait("gpsimd", s_dv, dv_ab)
        emit("gpsimd", lambda e: e.dma_start(out=dab, in_=ABhl[:]), s_gd, 16)
        dabt = nc.dram_tensor("dbg_abt", [128, 4, 1024], BF16, kind="ExternalOutput").ap()
        wait("gpsimd", s_dv, dv_tp)
        emit("gpsimd", lambda e: e.dma_start(out=dabt, in_=ABthl[:]), s_gd, 16)
        dwhl = nc.dram_tensor("dbg_whl", [128, 4, 8, 1024], BF16, kind="ExternalOutput").ap()
        wait("gpsimd", s_dv, dv_wsp)
        emit("gpsimd", lambda e: e.dma_start(out=dwhl, in_=whl[:]), s_gd, 16)
    wait("gpsimd", s_pe, pe_st1)
    gd_c2 = emit("gpsimd", lambda e: e.dma_start(
        out=whl[:], in_=c2s.rearrange("s (c p) n -> p s c n", p=128)), s_gd, 16)
    wait("tensor", s_gd, gd_c2)
    wait("tensor", s_dv, dv_tp)

    def st2(e):
        last = None
        terms2 = ((0, 0), (0, 1), (1, 0), (2, 2), (2, 3), (3, 2))
        for n2 in range(2):
            nsl = slice(512 * n2, 512 * n2 + 512)
            for ti, (ai, ci) in enumerate(terms2):
                for kc in range(8):
                    last = e.matmul(Mps[:, nsl],
                                    ABthl[:, ai, 128 * kc:128 * kc + 128],
                                    whl[:, ci, kc, nsl],
                                    start=(ti == 0 and kc == 0),
                                    stop=(ti == 5 and kc == 7))
        return last
    pe_st2 = emit("tensor", st2, s_pe)

    wait("vector", s_pe, pe_st2)
    dv_m = emit("vector", lambda e: e.tensor_copy(msl[:], Mps[:]), s_dv)
    wait("gpsimd", s_dv, dv_m)
    emit("gpsimd", lambda e: e.dma_start(out=ccin.ap(), in_=msl[:]), s_gd, 16)
    wait("gpsimd", s_gd, s_gd.v)
    s_cc.v += 16
    on("gpsimd", lambda e: e.collective_compute(
        "AllGather", OP.bypass, replica_groups=[list(range(8))],
        ins=[ccin.ap().opt()], outs=[ccout.ap().opt()]).then_inc(s_cc.h))
    wait("gpsimd", s_cc, 1)

    wait("gpsimd", s_pe, pe_st2)
    emit("gpsimd", lambda e: e.dma_start(
        out=whl[:, 0, :, :], in_=WuT.rearrange("(c p) n -> p c n", p=128)), s_gd, 16)
    wait("gpsimd", s_gd, s_gd.v)
    gd_wu = emit("gpsimd", lambda e: e.dma_start(
        out=wuh.rearrange("o p c t -> p c o t"),
        in_=whl[:, 0, :, :].rearrange("p c (o t) -> p c o t", o=8)), s_gd, 16)

    prep_sbuf.close()
    prep_psum.close()

    Mt = sb("Mt", [128, 8, 1024], BF16)
    Mt_l = sb("Mt_l", [128, 8, 1024], BF16)
    mtf32 = sb("mtf32", [128, 8, 1024], F32)
    gd_mt = emit("gpsimd", lambda e: e.dma_start(
        out=Mt[:], in_=ccout.ap().rearrange("(c p) n -> p c n", p=128)), s_gd, 16)
    gd_mtf = emit("gpsimd", lambda e: e.dma_start(
        out=mtf32[:], in_=ccout.ap().rearrange("(c p) n -> p c n", p=128)), s_gd, 16)
    wait("vector", s_gd, gd_mtf)
    dv_mtl = emit("vector", lambda e: e.tensor_sub(
        Mt_l[:].rearrange("p c n -> p (c n)"),
        mtf32[:].rearrange("p c n -> p (c n)"),
        Mt[:].rearrange("p c n -> p (c n)")), s_dv)

    kstg = sb("kstg", [128, 8, TW], BF16)
    vstg = sb("vstg", [128, 2, 8, 2, 65], BF16)
    kst = sb("kst", [128, NTP, 256], BF16)
    vsw = sb("vsw", [128, NKT, 2, 65], BF16)
    q_sb = sb("q_sb", [128, 512], BF16)
    P_sb = [sb("P0", [128, 1024], BF16), sb("P1", [128, 1024], BF16)]
    osb = sb("osb", [128, 1024], F32)
    on_sb = sb("on_sb", [128, 8, 512], BF16)
    x3 = sb("x3", [128, 8, 512], F32)
    xb3 = sb("xb3", [128, 8, 512], BF16)
    xbl3 = sb("xbl3", [128, 8, 512], BF16)
    x2r = [sb("x2r0", [128, 512], F32), sb("x2r1", [128, 512], F32)]
    o4 = [sb("o40", [128, 512], F32), sb("o41", [128, 512], F32)]
    wut = [sb("wut0", [128, 8, 128], BF16), sb("wut1", [128, 8, 128], BF16)]
    tmpB = [sb("tmpB0", [64, 512], BF16), sb("tmpB1", [64, 512], BF16)]

    PA = es.enter_context(nc.psum_tensor("PA", [128, 1024], F32))
    PB = es.enter_context(nc.psum_tensor("PB", [128, 1024], F32))
    PC = es.enter_context(nc.psum_tensor("PC", [128, 1024], F32))
    PD = es.enter_context(nc.psum_tensor("PD", [128, 1024], F32))
    LXs, LXb = PD[:, 0:512], PD[:, 512:1024]
    LYs, LYb = PC[:, 0:512], PC[:, 512:1024]
    Y1v = [PB[:, 0:256], PB[:, 512:768]]
    VPv = [PA[:, 0:512], PA[:, 512:1024]]
    SC = [PA, PB]
    oAv = PC[0:65, 0:512]
    oBv = PC[0:65, 512:1024]
    pjv = PD[:, 0:512]
    L3v = PD[:, 512:1024]

    wait("vector", s_gd, gd_wu)
    for _kh in range(2):
        for _pr in range(8):
            for _hh in range(2):
                on("vector", lambda e, a=_kh, b=_pr, c=_hh: e.memset(
                    vstg[:, a, b, c, 64:65], 1.0))
    dv_vones = emit("vector", lambda e: e.engine_nop(), s_dv)
    mark_x2out = [0] * NTILE
    mark_xl1out = [0] * NTILE
    mark_kout = [0] * NTILE
    mark_vout = [0] * NTILE
    mark_finX = [0] * NTILE
    mark_appX = [0] * NTILE
    mark_finY = [0] * NTILE
    mark_appY = [0] * NTILE
    mark_vevac = [0] * NTILE
    mark_peEnd = [0] * NTILE

    def emit_ln(xsrc, bank_s, bank_b, out_dst, fin_prev, app_prev, split=False):
        dv_cast = emit("vector", lambda e: e.tensor_copy(
            xbf[:].rearrange("p c t -> p (c t)"),
            xsrc[:].rearrange("p c t -> p (c t)")), s_dv)
        wait("tensor", s_dv, dv_cast)
        if fin_prev:
            wait("tensor", s_dv, fin_prev)

        def chain(e, half):
            last = None
            for kc in range(8):
                last = e.matmul(bank_s[32:33, TW * half:TW * half + TW],
                                ones_sb[:, 0:1],
                                xbf[:, kc, :], start=(kc == 0), stop=(kc == 7),
                                tile_position=(0, 32))
            return last
        pe_c1 = emit("tensor", lambda e: chain(e, 0), s_pe)
        wait("vector", s_pe, pe_c1)
        dv_sq = emit("vector", lambda e: e.tensor_mul(
            xbf[:].rearrange("p c t -> p (c t)"),
            xbf[:].rearrange("p c t -> p (c t)"),
            xbf[:].rearrange("p c t -> p (c t)")), s_dv)
        wait("tensor", s_dv, dv_sq)
        pe_c2 = emit("tensor", lambda e: chain(e, 1), s_pe)
        wait("vector", s_pe, pe_c2)
        on("vector", lambda e: e.tensor_scalar_mul(m1r[:, 0:TW], bank_s[32:33, 0:TW], RD))
        on("vector", lambda e: e.tensor_scalar_mul(v1r[:, 0:TW], bank_s[32:33, TW:2 * TW], RD))
        on("vector", lambda e: e.tensor_mul(q1r[:, 0:TW], m1r[:, 0:TW], m1r[:, 0:TW]))
        fin = emit("vector", lambda e: e.tensor_sub(v1r[:, 0:TW], v1r[:, 0:TW],
                                                    q1r[:, 0:TW]), s_dv)
        on("vector", lambda e: e.tensor_scalar_add(v1r[:, 0:TW], v1r[:, 0:TW], EPS))
        wait("scalar", s_dv, fin)
        on("scalar", lambda e: e.activation(v1r[:, 0:TW], v1r[:, 0:TW], AF.Ln))
        ac_ln = emit("scalar", lambda e: e.engine_nop() if False else e.activation(
            q1r[:, 0:TW], v1r[:, 0:TW], AF.Copy), s_ac)
        wait("vector", s_ac, ac_ln)
        dv_half = emit("vector", lambda e: e.tensor_scalar_mul(
            q1r[:, 0:TW], q1r[:, 0:TW], -0.5), s_dv)
        wait("scalar", s_dv, dv_half)
        ac_rs = emit("scalar", lambda e: e.activation(r1r[:, 0:TW], q1r[:, 0:TW],
                                                      AF.Exp), s_ac)
        wait("vector", s_ac, ac_rs)
        on("vector", lambda e: e.tensor_copy(mh1[:, 0:TW], m1r[:, 0:TW]))
        on("vector", lambda e: e.tensor_copy(rh1[:, 0:TW], r1r[:, 0:TW]))
        dv_sp = emit("vector", lambda e: e.tensor_sub(rl1[:, 0:TW], r1r[:, 0:TW],
                                                      rh1[:, 0:TW]), s_dv)
        wait("tensor", s_dv, dv_sp)
        if app_prev:
            wait("tensor", s_dv, app_prev)

        def bc(e):
            e.matmul(bank_b[:, 0:TW], ones_sb[32:33, :], mh1[:, 0:TW],
                     start=True, stop=True, tile_position=(32, 0))
            e.matmul(bank_b[:, TW:2 * TW], ones_sb[32:33, :], rh1[:, 0:TW],
                     start=True, stop=False, tile_position=(32, 0))
            return e.matmul(bank_b[:, TW:2 * TW], ones_sb[32:33, :], rl1[:, 0:TW],
                            start=False, stop=True, tile_position=(32, 0))
        pe_bc = emit("tensor", bc, s_pe)
        wait("vector", s_pe, pe_bc)
        for kc in range(8):
            on("vector", lambda e, k=kc: e.tensor_sub(xf1[:, k, :],
                                                      xsrc[:, k, :],
                                                      bank_b[:, 0:TW]))
        app = None
        if not split:
            for kc in range(8):
                app = emit("vector", lambda e, k=kc: e.tensor_mul(
                    out_dst(k), xf1[:, k, :], bank_b[:, TW:2 * TW]),
                    s_dv if kc == 7 else None)
        else:
            for kc in range(8):
                on("vector", lambda e, k=kc: e.tensor_mul(
                    xf1[:, k, :], xf1[:, k, :], bank_b[:, TW:2 * TW]))
            for kc in range(8):
                on("vector", lambda e, k=kc: e.tensor_copy(out_dst(k),
                                                           xf1[:, k, :]))
            for kc in range(8):
                app = emit("vector", lambda e, k=kc: e.tensor_sub(
                    xls[:, k, :], xf1[:, k, :], out_dst(k)),
                    s_dv if kc == 7 else None)
        return fin, app

    for t in range(NTILE):
        p = t % 2
        tok = slice(TW * t, TW * t + TW)
        if t >= 2:
            wait("sync", s_gd, mark_x2out[t - 2])
            wait("sync", s_dv, mark_appY[t - 2])
        ld_x = emit("sync", lambda e, pp=p, tk=tok: e.dma_start(
            out=xt[pp][:], in_=xT[:, tk].rearrange("(c p) t -> p c t", p=128)),
            s_ld, 16)
        wait("vector", s_ld, ld_x)
        if t == 0:
            wait("tensor", s_gd, gd_wu)
            wait("vector", s_gd, gd_mt)
        else:
            wait("vector", s_pe, mark_peEnd[t - 1])
        if t >= 1:
            wait("vector", s_sd, mark_xl1out[t - 1])
        fin, app = emit_ln(xt[p], LXs, LXb, lambda k: xbf[:, k, :],
                           mark_finX[t - 1] if t else 0,
                           mark_appX[t - 1] if t else 0, split=True)
        mark_finX[t], mark_appX[t] = fin, app
        wait("scalar", s_dv, app)
        emit("scalar", lambda e, tk=tok: e.dma_start(
            out=xl1h[:, tk].rearrange("(c p) t -> p c t", p=128), in_=xbf[:]),
            s_sd, 16)
        mark_xl1out[t] = emit("scalar", lambda e, tk=tok: e.dma_start(
            out=xl1l[:, tk].rearrange("(c p) t -> p c t", p=128), in_=xls[:]),
            s_sd, 16)

        wait("tensor", s_dv, app)
        stt = []
        for oc in range(8):
            y = Y1v[oc % 2]
            if oc >= 2:
                wait("tensor", s_dv, stt[oc - 2])

            def ffn1(e, o=oc, yy=y):
                last = None
                for kc in range(8):
                    last = e.matmul(yy, Mt[:, kc, 128 * o:128 * o + 128],
                                    xbf[:, kc, :], start=(kc == 0), stop=(kc == 7))
                return last
            pe_f = emit("tensor", ffn1, s_pe)
            wait("vector", s_pe, pe_f)
            stt.append(emit("vector", lambda e, o=oc, yy=y, pp=p: e.scalar_tensor_tensor(
                xt[pp][:, o, :], yy, fb2_sb[:, o:o + 1], xt[pp][:, o, :],
                OP.add, OP.add), s_dv))

        wait("gpsimd", s_dv, stt[-1])
        mark_x2out[t] = emit("gpsimd", lambda e, pp=p, tk=tok: e.dma_start(
            out=x2h[:, tk].rearrange("(c p) t -> p c t", p=128), in_=xt[pp][:]),
            s_gd, 16)

        wait("vector", s_sd, mark_kout[t - 1] if t else 0)
        wait("vector", s_pe, mark_peEnd[t - 1] if t else 0)
        fin2, app2 = emit_ln(xt[p], LYs, LYb, lambda k: kstg[:, k, :],
                             mark_finY[t - 1] if t else 0,
                             mark_appY[t - 1] if t else 0)
        mark_finY[t], mark_appY[t] = fin2, app2

        wait("scalar", s_dv, app2)
        for pr in range(NPAIR):
            mark_kout[t] = emit("scalar", lambda e, r=pr, tt=t: e.dma_start(
                out=keysh[r, tt], in_=kstg[:, r, :]), s_sd, 16)

        wait("tensor", s_dv, app2)
        if t:
            wait("vector", s_sd, mark_vout[t - 1])
            wait("tensor", s_dv, mark_vevac[t - 1])
        ev = []
        for kh in range(2):
            for g in range(2):
                vp = VPv[(2 * kh + g) % 2]
                if len(ev) >= 2:
                    wait("tensor", s_dv, ev[-2])

                def vproj(e, k=kh, gg=g, vv=vp):
                    last = None
                    for i in range(4):
                        pr = 4 * gg + i
                        last = e.matmul(vv[:, 128 * i:128 * i + 128],
                                        kstg[:, pr, 128 * k:128 * k + 128], Wv2[:],
                                        start=True, stop=True)
                    return last
                pe_v = emit("tensor", vproj, s_pe)
                wait("vector", s_pe, pe_v)
                for i4 in range(4):
                    mk_e = emit("vector", lambda e, k=kh, gg=g, i=i4, vv=vp:
                                e.tensor_copy(
                                    vstg[:, k, 4 * gg + i, :, 0:64],
                                    vv[:, 128 * i:128 * i + 128].rearrange(
                                        "p (h c) -> p h c", h=2)), s_dv)
                ev.append(mk_e)
        mark_vevac[t] = ev[-1]
        wait("scalar", s_dv, ev[-1])
        for pr in range(NPAIR):
            mark_vout[t] = emit("scalar", lambda e, r=pr, tt=t: e.dma_start(
                out=vh[r, 2 * tt:2 * tt + 2].rearrange("k p f -> p k f"),
                in_=vstg[:, :, r, :, :].rearrange("p k h c -> p k (h c)")), s_sd, 16)
        mark_peEnd[t] = s_pe.v

    import kernel as _km2
    if getattr(_km2, "DEBUG_DUMP", False):
        dx2 = nc.dram_tensor("dbg_x2h", [D, T], F32, kind="ExternalOutput").ap()
        dke = nc.dram_tensor("dbg_keysh", [NPAIR, NTP, 128, 256], BF16,
                             kind="ExternalOutput").ap()
        dvh = nc.dram_tensor("dbg_vh", [NPAIR, NKT, 128, 130], BF16,
                             kind="ExternalOutput").ap()
        wait("gpsimd", s_sd, s_sd.v)
        wait("gpsimd", s_gd, s_gd.v)
        emit("gpsimd", lambda e: e.dma_start(out=dx2, in_=x2h), s_gd, 16)
        emit("gpsimd", lambda e: e.dma_start(out=dke, in_=keysh), s_gd, 16)
        emit("gpsimd", lambda e: e.dma_start(out=dvh, in_=vh), s_gd, 16)
        dmm = nc.dram_tensor("dbg_mm", [D, D], F32, kind="ExternalOutput").ap()
        emit("gpsimd", lambda e: e.dma_start(out=dmm, in_=ccout.ap()), s_gd, 16)
        dci = nc.dram_tensor("dbg_ci", [128, D], F32, kind="ExternalOutput").ap()
        emit("gpsimd", lambda e: e.dma_start(out=dci, in_=ccin.ap()), s_gd, 16)
        dvs = nc.dram_tensor("dbg_vstg", [128, 2, 8, 2, 65], BF16,
                             kind="ExternalOutput").ap()
        emit("gpsimd", lambda e: e.dma_start(out=dvs, in_=vstg[:]), s_gd, 16)
        wait("gpsimd", s_gd, s_gd.v)
    BAR = dict(ld=s_ld.v, sd=s_sd.v, gd=s_gd.v, pe=s_pe.v, dv=s_dv.v, ac=s_ac.v)

    with nc.Block() as block:
        for ename, meth in (("sync", block.sync), ("tensor", block.tensor),
                            ("vector", block.vector), ("scalar", block.scalar),
                            ("gpsimd", block.gpsimd)):
            def runner(eng, _steps=plan[ename]):
                for f in _steps:
                    f(eng)
            meth(runner)

        S_dict = dict(Mt=Mt, G2=G2, Wv2=Wv2, tri_sb=tri_sb, eye_sb=eye_sb,
                      ones_sb=ones_sb, buT_sb=buT_sb, fb2_sb=fb2_sb,
                      m1r=m1r, v1r=v1r, q1r=q1r, r1r=r1r, mh1=mh1,
                      rh1=rh1, rl1=rl1, rzr=rzr, rzf=rzf, tmpB=tmpB,
                      xf1=xf1, kst=kst, vsw=vsw, Mt_l=Mt_l, xbl3=xbl3,
                      q_sb=q_sb, P_sb=P_sb, osb=osb, on_sb=on_sb, x3=x3,
                      xb3=xb3, x2r=x2r, o4=o4, wut=wut,
                      SC=SC, oAv=oAv, oBv=oBv, pjv=pjv, L3v=L3v)
        H_dict = dict(keysh=keysh, vh=vh, x2h=x2h, wuh=wuh, out=out,
                      xl1h=xl1h, xl1l=xl1l)
        sems_d = dict(s_ld=s_ld, s_sd=s_sd, s_gd=s_gd, s_pe=s_pe,
                      s_dv=s_dv, s_ac=s_ac)

        class _SkipInst:
            def then_inc(self, *a, **k):
                return self

        _SKIP = _SkipInst()

        class _EngProxy:
            def __init__(self, real, on):
                self._real = real
                self._on = on

            def __getattr__(self, name):
                if self._on:
                    return getattr(self._real, name)
                return lambda *a, **k: _SKIP

        def _one_engine_switch(eng):
            pid = nc.partition_id(engines=[eng.engine])
            reg = eng.compute_val(pid % 2)
            names = ("tensor", "vector", "scalar", "sync")
            reals = (nc.tensor, nc.vector, nc.scalar, nc.sync)
            for arm in nc.Switch(engines=[eng], index=[reg], n=2):
                proxies = [_EngProxy(r, r is eng) for r in reals]
                _emit_arm(nc, arm, proxies, BAR, sems_d, S_dict, H_dict)

        @block.tensor
        def _(tensor):
            _one_engine_switch(nc.tensor)

        @block.vector
        def _(vector):
            _one_engine_switch(nc.vector)

        @block.scalar
        def _(scalar):
            _one_engine_switch(nc.scalar)

        @block.sync
        def _(sync):
            _one_engine_switch(nc.sync)

    es.close()
    return nc


ARM_LVL = 9
UNIT_BARRIER = True


def _emit_arm(nc, arm, engines, BAR, sems, S, H):
    import kernel as _self
    LVL = getattr(_self, "ARM_LVL", 9)
    tensor, vector, scalar, sync = engines
    s_ld, s_sd, s_gd, s_pe, s_dv, s_ac = (sems["s_ld"], sems["s_sd"],
                                          sems["s_gd"], sems["s_pe"],
                                          sems["s_dv"], sems["s_ac"])
    ld = BAR["ld"]; sd = BAR["sd"]; gd = BAR["gd"]
    pe = BAR["pe"]; dv = BAR["dv"]; ac = BAR["ac"]

    Mt, G2, Wv2 = S["Mt"], S["G2"], S["Wv2"]
    tri_sb, ones_sb = S["tri_sb"], S["ones_sb"]
    buT_sb, fb2_sb = S["buT_sb"], S["fb2_sb"]
    m1r, v1r, q1r, r1r = S["m1r"], S["v1r"], S["q1r"], S["r1r"]
    mh1, rh1, rl1, rzr = S["mh1"], S["rh1"], S["rl1"], S["rzr"]
    rzf = S["rzf"]
    tmpB = S["tmpB"]
    xf1 = S["xf1"]
    Mt_l = S["Mt_l"]
    xbl3 = S["xbl3"]
    kst, vsw, q_sb, P_sb = S["kst"], S["vsw"], S["q_sb"], S["P_sb"]
    osb, on_sb, x3, xb3 = S["osb"], S["on_sb"], S["x3"], S["xb3"]
    x2r, o4, wut = S["x2r"], S["o4"], S["wut"]
    SC, oAv, oBv, pjv, L3v = S["SC"], S["oAv"], S["oBv"], S["pjv"], S["L3v"]
    keysh, vh, x2h, wuh, out = H["keysh"], H["vh"], H["x2h"], H["wuh"], H["out"]
    xl1h, xl1l = H["xl1h"], H["xl1l"]

    for e in engines:
        e.wait_ge(s_ld.h, BAR["ld"])
        e.wait_ge(s_sd.h, BAR["sd"])
        e.wait_ge(s_gd.h, BAR["gd"])
        e.wait_ge(s_pe.h, BAR["pe"])
        e.wait_ge(s_dv.h, BAR["dv"])
        e.wait_ge(s_ac.h, BAR["ac"])

    units = UNITS[arm]
    m_kst_free_pe = 0
    m_vsw_free_pe = 0
    m_sc_exp = {}
    m_sc_dvq = 0
    m_oA_dv = 0
    m_on_pe = 0
    m_x2r_dv = [0, 0]
    m_tmpB_ld = [0, 0]
    m_xb3_free_pe = 0
    m_o4_gd = [0, 0]
    m_o4_dv = [0, 0]
    m_wut_pe = [0, 0]
    m_pj_dv = 0
    m_L3_dv = 0
    m_x3_dv = 0
    exp_done_prev_pair = 0

    UB = getattr(_self, "UNIT_BARRIER", False)
    for ui, u in enumerate(units):
        if UB and ui:
            for e in (tensor, vector, scalar, sync):
                e.wait_ge(s_pe.h, pe)
                e.wait_ge(s_dv.h, dv)
                e.wait_ge(s_ac.h, ac)
                e.wait_ge(s_ld.h, ld)
                e.wait_ge(s_sd.h, sd)
        nkt = 4 * u + 4
        ntp = 2 * u + 2
        slot = ui
        for pr in range(NPAIR):
            if m_kst_free_pe:
                sync.wait_ge(s_pe.h, m_kst_free_pe)
            ld += 16
            sync.dma_start(out=kst[:, 0:ntp, :],
                           in_=keysh[pr, 0:ntp].rearrange("k p t -> p k t")
                           ).then_inc(s_ld.h, 16)
            ld_k = ld
            if m_vsw_free_pe:
                sync.wait_ge(s_pe.h, m_vsw_free_pe)
            ld += 16
            sync.dma_start(out=vsw[:, 0:nkt, :, :].rearrange("p k h c -> p k (h c)"),
                           in_=vh[pr, 0:nkt].rearrange("k p f -> p k f")
                           ).then_inc(s_ld.h, 16)
            ld_v = ld

            av_pe_done = {}
            if LVL < 1:
                continue
            tensor.wait_ge(s_ld.h, ld_k)
            if m_sc_exp.get(0):
                tensor.wait_ge(s_ac.h, m_sc_exp[0])
            if m_sc_dvq:
                tensor.wait_ge(s_dv.h, m_sc_dvq)
            for half in range(2):
                pe += 1
                tensor.matmul(SC[0][:, 512 * 0 + 256 * half:256 * half + 256 + 0],
                              G2[:], kst[:, 2 * u + half, :],
                              start=True, stop=True).then_inc(s_pe.h, 1)
            vector.wait_ge(s_pe.h, pe)
            dv += 1
            vector.tensor_copy(q_sb[:], SC[0][:, 0:512]).then_inc(s_dv.h, 1)
            m_sc_dvq = dv
            q_ready_dv = dv

            av_mark = {}

            def emit_av(kt):
                nonlocal pe
                d2 = kt - 4 * u
                qo2 = 128 * d2 if d2 > 0 else 0
                tensor.wait_ge(s_ac.h, av_mark[kt])
                if kt == 0:
                    tensor.wait_ge(s_ld.h, ld_v)
                    if m_oA_dv:
                        tensor.wait_ge(s_dv.h, m_oA_dv)
                pe += 1
                tensor.matmul(oAv[:, qo2:512], vsw[:, kt, 0, :],
                              P_sb[kt % 2][:, qo2:512], start=(kt == 0),
                              stop=(kt == nkt - 1)).then_inc(s_pe.h, 1)
                pe += 1
                tensor.matmul(oBv[:, qo2:512], vsw[:, kt, 1, :],
                              P_sb[kt % 2][:, 512 + qo2:1024], start=(kt == 0),
                              stop=(kt == nkt - 1)).then_inc(s_pe.h, 1)

            for kt in range(nkt if LVL >= 2 else 0):
                d = kt - 4 * u
                qoff = 128 * d if d > 0 else 0
                sc = SC[kt % 2]
                scm_prev = m_sc_exp.get(kt % 2, 0)
                tensor.wait_ge(s_dv.h, q_ready_dv)
                if scm_prev:
                    tensor.wait_ge(s_ac.h, scm_prev)
                pe += 1
                tensor.matmul(sc[0:128, qoff:512],
                              kst[0:64, kt // 2, 128 * (kt % 2):128 * (kt % 2) + 128],
                              q_sb[0:64, qoff:512], start=True, stop=True,
                              tile_position=(0, 0)).then_inc(s_pe.h, 1)
                pe += 1
                tensor.matmul(sc[0:128, 512 + qoff:1024],
                              kst[64:128, kt // 2, 128 * (kt % 2):128 * (kt % 2) + 128],
                              q_sb[64:128, qoff:512], start=True, stop=True,
                              tile_position=(64, 0)).then_inc(s_pe.h, 1)
                pe_sc = pe
                mask_dv = 0
                if d >= 0 and LVL >= 3:
                    vector.wait_ge(s_pe.h, pe_sc)
                    dv += 1
                    vector.tensor_add(
                        sc[:, qoff:qoff + 128].rearrange("p (s t) -> p s t", s=1),
                        sc[:, qoff:qoff + 128].rearrange("p (s t) -> p s t", s=1),
                        tri_sb[:].rearrange("p (s t) -> p s t", s=1)
                    ).then_inc(s_dv.h, 1)
                    dv += 1
                    vector.tensor_add(
                        sc[:, 512 + qoff:512 + qoff + 128],
                        sc[:, 512 + qoff:512 + qoff + 128],
                        tri_sb[:]).then_inc(s_dv.h, 1)
                    mask_dv = dv
                if LVL < 4:
                    continue
                scalar.wait_ge(s_pe.h, pe_sc)
                if mask_dv:
                    scalar.wait_ge(s_dv.h, mask_dv)
                if kt >= 2:
                    scalar.wait_ge(s_pe.h, av_pe_done.get(kt - 2, 0))
                ac += 1
                scalar.activation(
                    P_sb[kt % 2][:].rearrange("p (s q) -> p s q", s=2)[:, :, qoff:512],
                    sc.rearrange("p (s q) -> p s q", s=2)[:, :, qoff:512],
                    AF.Exp).then_inc(s_ac.h, 1)
                m_sc_exp[kt % 2] = ac
                av_mark[kt] = ac
                if LVL < 5:
                    continue
                if kt >= 1:
                    emit_av(kt - 1)
                    av_pe_done[kt - 1] = pe
            if LVL >= 5 and nkt >= 1:
                emit_av(nkt - 1)
            m_kst_free_pe = pe
            m_vsw_free_pe = pe
            pe_av_done = pe

            if LVL < 6:
                continue
            vector.wait_ge(s_pe.h, pe_av_done)
            vector.reciprocal(rzf[:, 0:512], oAv[64:65, :])
            vector.reciprocal(rzf[:, 512:1024], oBv[64:65, :])
            dv += 1
            vector.tensor_copy(rzr[:], rzf[:]).then_inc(s_dv.h, 1)
            rz_dv = dv
            dv += 1
            vector.tensor_copy(osb[0:64, 0:512], oAv[0:64, :]).then_inc(s_dv.h, 1)
            dv += 1
            vector.tensor_copy(osb[0:64, 512:1024], oBv[0:64, :]).then_inc(s_dv.h, 1)
            m_oA_dv = dv
            tensor.wait_ge(s_dv.h, rz_dv)
            if m_sc_exp.get(0):
                tensor.wait_ge(s_ac.h, m_sc_exp[0])
            if m_sc_exp.get(1):
                tensor.wait_ge(s_ac.h, m_sc_exp[1])
            if m_sc_dvq:
                tensor.wait_ge(s_dv.h, m_sc_dvq)
            pe += 1
            tensor.matmul(SC[0][0:64, 0:512], ones_sb[64:65, 0:64], rzr[:, 0:512],
                          start=True, stop=True,
                          tile_position=(64, 0)).then_inc(s_pe.h, 1)
            pe += 1
            tensor.matmul(SC[0][0:64, 512:1024], ones_sb[64:65, 0:64],
                          rzr[:, 512:1024], start=True, stop=True,
                          tile_position=(64, 0)).then_inc(s_pe.h, 1)
            pe_rzb = pe
            vector.wait_ge(s_pe.h, pe_rzb)
            vector.wait_ge(s_dv.h, m_oA_dv)
            if m_on_pe:
                vector.wait_ge(s_pe.h, m_on_pe)
            if m_tmpB_ld[pr % 2]:
                vector.wait_ge(s_ld.h, m_tmpB_ld[pr % 2])
            dv += 1
            vector.tensor_mul(on_sb[0:64, pr, :], osb[0:64, 0:512],
                              SC[0][0:64, 0:512]).then_inc(s_dv.h, 1)
            dv += 1
            vector.tensor_mul(tmpB[pr % 2][:], osb[0:64, 512:1024],
                              SC[0][0:64, 512:1024]).then_inc(s_dv.h, 1)
            m_sc_dvq = dv
            sync.wait_ge(s_dv.h, dv)
            ld += 16
            sync.dma_start(out=on_sb[64:128, pr, :],
                           in_=tmpB[pr % 2][:]).then_inc(s_ld.h, 16)
            m_tmpB_ld[pr % 2] = ld
            on_ready_dv = dv
            on_ready_ld = ld
            exp_done_prev_pair = ac

        if LVL < 7:
            continue
        if m_xb3_free_pe:
            sync.wait_ge(s_pe.h, m_xb3_free_pe)
        ld += 16
        sync.dma_start(out=xb3[:], in_=xl1h[:, 512 * u:512 * u + 512].rearrange(
            "(c p) t -> p c t", p=128)).then_inc(s_ld.h, 16)
        ld += 16
        sync.dma_start(out=xbl3[:], in_=xl1l[:, 512 * u:512 * u + 512].rearrange(
            "(c p) t -> p c t", p=128)).then_inc(s_ld.h, 16)
        ld_xl1 = ld
        grp_dv = []
        grp_pe = []
        for oc in range(8):
            par = oc % 2
            if m_wut_pe[par]:
                sync.wait_ge(s_pe.h, m_wut_pe[par])
            ld += 16
            sync.dma_start(out=wut[par][:], in_=wuh[oc]).then_inc(s_ld.h, 16)
            ld_wu = ld
            if m_x2r_dv[par]:
                sync.wait_ge(s_dv.h, m_x2r_dv[par])
            ld += 16
            sync.dma_start(out=x2r[par][:],
                           in_=x2h[:, 512 * u:512 * u + 512].rearrange(
                               "(c p) t -> p c t", p=128)[:, oc, :]).then_inc(s_ld.h, 16)
            ld_x2 = ld

            bank = pjv if oc % 2 == 0 else L3v
            tensor.wait_ge(s_ld.h, max(ld_wu, on_ready_ld, ld_xl1))
            tensor.wait_ge(s_dv.h, on_ready_dv)
            if len(grp_dv) >= 2:
                tensor.wait_ge(s_dv.h, grp_dv[oc - 2])
            if oc == 0 and m_pj_dv:
                tensor.wait_ge(s_dv.h, m_pj_dv)
            if oc == 1 and m_L3_dv:
                tensor.wait_ge(s_dv.h, m_L3_dv)
            ocs = slice(128 * oc, 128 * oc + 128)
            for pr in range(NPAIR):
                tensor.matmul(bank, wut[par][:, pr, :], on_sb[:, pr, :],
                              start=(pr == 0), stop=False)
            for kc in range(8):
                tensor.matmul(bank, Mt_l[:, kc, ocs], xb3[:, kc, :],
                              start=False, stop=False)
            last = None
            for kc in range(8):
                last = tensor.matmul(bank, Mt[:, kc, ocs], xbl3[:, kc, :],
                                     start=False, stop=(kc == 7))
            pe += 1
            last.then_inc(s_pe.h, 1)
            grp_pe.append(pe)
            m_wut_pe[par] = pe
            vector.wait_ge(s_pe.h, pe)
            vector.wait_ge(s_ld.h, ld_x2)
            dv += 1
            vector.scalar_tensor_tensor(x3[:, oc, :], bank, buT_sb[:, oc:oc + 1],
                                        x2r[par][:], OP.add,
                                        OP.add).then_inc(s_dv.h, 1)
            grp_dv.append(dv)
            m_x2r_dv[par] = dv
        m_on_pe = grp_pe[-1]
        x3_dv = grp_dv[-1]
        m_pj_dv = grp_dv[-2]
        m_L3_dv = grp_dv[-1]

        if LVL < 8:
            continue
        W3 = 512
        vector.wait_ge(s_dv.h, x3_dv)
        dv += 1
        vector.tensor_copy(xb3[:].rearrange("p c t -> p (c t)"),
                           x3[:].rearrange("p c t -> p (c t)")).then_inc(s_dv.h, 1)
        cast3 = dv
        tensor.wait_ge(s_dv.h, cast3)
        tensor.wait_ge(s_dv.h, grp_dv[-2])

        def chain3(bank):
            last = None
            for kc in range(8):
                last = tensor.matmul(bank[32:33, 0:W3], ones_sb[:, 0:1],
                                     xb3[:, kc, :], start=(kc == 0), stop=(kc == 7),
                                     tile_position=(0, 32))
            return last
        tensor.wait_ge(s_dv.h, grp_dv[-1])
        pe += 1
        chain3(pjv).then_inc(s_pe.h, 1)
        vector.wait_ge(s_pe.h, pe)
        dv += 1
        vector.tensor_mul(xb3[:].rearrange("p c t -> p (c t)"),
                          xb3[:].rearrange("p c t -> p (c t)"),
                          xb3[:].rearrange("p c t -> p (c t)")).then_inc(s_dv.h, 1)
        tensor.wait_ge(s_dv.h, dv)
        pe += 1
        chain3(L3v).then_inc(s_pe.h, 1)
        vector.wait_ge(s_pe.h, pe)
        vector.tensor_scalar_mul(m1r[:, 0:W3], pjv[32:33, 0:W3], RD)
        vector.tensor_scalar_mul(v1r[:, 0:W3], L3v[32:33, 0:W3], RD)
        vector.tensor_mul(q1r[:, 0:W3], m1r[:, 0:W3], m1r[:, 0:W3])
        dv += 1
        vector.tensor_sub(v1r[:, 0:W3], v1r[:, 0:W3], q1r[:, 0:W3]).then_inc(s_dv.h, 1)
        vector.tensor_scalar_add(v1r[:, 0:W3], v1r[:, 0:W3], EPS)
        scalar.wait_ge(s_dv.h, dv)
        scalar.activation(v1r[:, 0:W3], v1r[:, 0:W3], AF.Ln)
        ac += 1
        scalar.activation(q1r[:, 0:W3], v1r[:, 0:W3], AF.Copy).then_inc(s_ac.h, 1)
        vector.wait_ge(s_ac.h, ac)
        dv += 1
        vector.tensor_scalar_mul(q1r[:, 0:W3], q1r[:, 0:W3],
                                 -0.5).then_inc(s_dv.h, 1)
        scalar.wait_ge(s_dv.h, dv)
        ac += 1
        scalar.activation(r1r[:, 0:W3], q1r[:, 0:W3], AF.Exp).then_inc(s_ac.h, 1)
        vector.wait_ge(s_ac.h, ac)
        vector.tensor_copy(mh1[:, 0:W3], m1r[:, 0:W3])
        vector.tensor_copy(rh1[:, 0:W3], r1r[:, 0:W3])
        dv += 1
        vector.tensor_sub(rl1[:, 0:W3], r1r[:, 0:W3], rh1[:, 0:W3]).then_inc(s_dv.h, 1)
        tensor.wait_ge(s_dv.h, dv)
        tensor.matmul(L3v[:, 0:W3], ones_sb[32:33, :], mh1[:, 0:W3],
                      start=True, stop=True, tile_position=(32, 0))
        tensor.matmul(pjv[:, 0:W3], ones_sb[32:33, :], rh1[:, 0:W3],
                      start=True, stop=False, tile_position=(32, 0))
        pe += 1
        tensor.matmul(pjv[:, 0:W3], ones_sb[32:33, :], rl1[:, 0:W3],
                      start=False, stop=True,
                      tile_position=(32, 0)).then_inc(s_pe.h, 1)
        bc3 = pe
        vector.wait_ge(s_pe.h, bc3)
        dv += 1
        last = None
        for hw in range(2):
            hsl = slice(256 * hw, 256 * hw + 256)
            for kc in range(8):
                vector.tensor_sub(xf1[:, kc, :], x3[:, kc, hsl], L3v[:, hsl])
                vector.tensor_mul(xf1[:, kc, :], xf1[:, kc, :], pjv[:, hsl])
                vector.tensor_copy(xb3[:, kc, hsl], xf1[:, kc, :])
                last = vector.tensor_sub(xbl3[:, kc, hsl], xf1[:, kc, :],
                                         xb3[:, kc, hsl])
        last.then_inc(s_dv.h, 1)
        app3 = dv
        m_pj_dv = app3
        m_L3_dv = app3

        if LVL < 9:
            continue
        tensor.wait_ge(s_dv.h, app3)
        f2dv = []
        for oc in range(8):
            bank = pjv if oc % 2 == 0 else L3v
            par = oc % 2
            if len(f2dv) >= 2:
                tensor.wait_ge(s_dv.h, f2dv[oc - 2])
            ocs2 = slice(128 * oc, 128 * oc + 128)
            for kc in range(8):
                tensor.matmul(bank, Mt[:, kc, ocs2], xb3[:, kc, :],
                              start=(kc == 0), stop=False)
            for kc in range(8):
                tensor.matmul(bank, Mt_l[:, kc, ocs2], xb3[:, kc, :],
                              start=False, stop=False)
            last = None
            for kc in range(8):
                last = tensor.matmul(bank, Mt[:, kc, ocs2], xbl3[:, kc, :],
                                     start=False, stop=(kc == 7))
            pe += 1
            last.then_inc(s_pe.h, 1)
            vector.wait_ge(s_pe.h, pe)
            if m_o4_gd[par]:
                vector.wait_ge(s_ld.h, m_o4_gd[par])
            dv += 1
            vector.scalar_tensor_tensor(o4[par][:], bank, fb2_sb[:, oc:oc + 1],
                                        x3[:, oc, :], OP.add,
                                        OP.add).then_inc(s_dv.h, 1)
            f2dv.append(dv)
            m_o4_dv[par] = dv
            sync.wait_ge(s_dv.h, dv)
            ld += 16
            sync.dma_start(
                out=out[:, 512 * slot:512 * slot + 512].rearrange(
                    "(c p) t -> p c t", p=128)[:, oc, :],
                in_=o4[par][:]).then_inc(s_ld.h, 16)
            m_o4_gd[par] = ld
        m_pj_dv = f2dv[-2]
        m_L3_dv = f2dv[-1]
        m_x3_dv = f2dv[-1]
        m_xb3_free_pe = pe

    sync.wait_ge(s_ld.h, ld)
    scalar.wait_ge(s_sd.h, sd)


_NC_CACHE = None


def _get_nc():
    global _NC_CACHE
    if _NC_CACHE is None:
        _NC_CACHE = _build()
    return _NC_CACHE


def kernel(L0_attn_x, L0_attn_Wq, L0_attn_Wk, L0_attn_Wv, L0_attn_Wu, L0_attn_bu,
           L0_ffn_w_freq_real, L0_ffn_w_freq_imag, L0_ffn_bias):
    x = np.asarray(L0_attn_x, np.float32)
    Wq = np.ascontiguousarray(np.asarray(L0_attn_Wq, np.float32))
    Wk = np.ascontiguousarray(np.asarray(L0_attn_Wk, np.float32))
    WvT_ = np.ascontiguousarray(np.asarray(L0_attn_Wv, np.float32).T)
    WuT_ = np.ascontiguousarray(np.asarray(L0_attn_Wu, np.float32).T)
    bu = np.asarray(L0_attn_bu, np.float32)
    wfr_ = np.ascontiguousarray(np.asarray(L0_ffn_w_freq_real, np.float32))
    wfi_ = np.ascontiguousarray(np.asarray(L0_ffn_w_freq_imag, np.float32))
    fb = np.asarray(L0_ffn_bias, np.float32)

    bfd = ml_dtypes.bfloat16

    def split_hl(a32):
        h = a32.astype(bfd)
        lo = (a32 - h.astype(np.float32)).astype(bfd)
        return h, lo

    idx = np.arange(D, dtype=np.int64)
    ang = (2.0 * np.pi / D) * ((idx[:, None] * idx[None, :]) % D).astype(np.float64)
    C = np.cos(ang)
    Sn = np.sin(ang)
    c2_h, c2_l = split_hl((C * (0.5 / D)).astype(np.float32))
    s2_h, s2_l = split_hl((-Sn * (0.5 / D)).astype(np.float32))
    c2s_ = np.ascontiguousarray(np.stack([c2_h, c2_l, s2_h, s2_l]))
    tri_ = np.where(np.arange(128)[:, None] > np.arange(128)[None, :],
                    np.float32(-1e30), np.float32(0.0)).astype(np.float32)
    buT_ = np.ascontiguousarray(bu.reshape(8, 128).T)
    fbT_ = np.ascontiguousarray(fb.reshape(8, 128).T)

    in_maps = []
    for c in range(8):
        b = c >> 1
        rows = slice(128 * c, 128 * c + 128)
        ch, cl = split_hl(C[:, rows].astype(np.float32))
        sh, sl = split_hl(Sn[:, rows].astype(np.float32))
        nh, nl = split_hl((-Sn[:, rows]).astype(np.float32))
        in_maps.append({
            "xT": np.ascontiguousarray(x[b].T),
            "wfr": wfr_, "wfi": wfi_,
            "c1s": np.ascontiguousarray(np.stack([ch, cl, sh, sl, nh, nl])),
            "c2s": c2s_,
            "wq": Wq, "wk": Wk, "wvT": WvT_, "WuT": WuT_,
            "buT": buT_, "fbT": fbT_, "tri": tri_,
        })

    nc = _get_nc()
    res = run_bass_kernel_spmd(nc, in_maps, core_ids=list(range(8)))

    y = np.empty((B, T, D), np.float32)
    for b in range(B):
        outA = res.results[2 * b]["out"]
        outB = res.results[2 * b + 1]["out"]
        yT = np.empty((D, T), np.float32)
        for slot, u in enumerate(UNITS[0]):
            yT[:, 512 * u:512 * u + 512] = outA[:, 512 * slot:512 * slot + 512]
        for slot, u in enumerate(UNITS[1]):
            yT[:, 512 * u:512 * u + 512] = outB[:, 512 * slot:512 * slot + 512]
        y[b] = yT.T
    return y
